# version 16
# baseline (speedup 1.0000x reference)
import os
import numpy as np
import ml_dtypes
import concourse.bass as bass
import concourse.mybir as mybir
from concourse.bass_utils import run_bass_kernel_spmd

F32 = mybir.dt.float32
BF16 = mybir.dt.bfloat16
AF = mybir.ActivationFunctionType
ALU = mybir.AluOpType
AX = mybir.AxisListType

D, IF, NTOK, TS = 2048, 8192, 8192, 1024
I32 = mybir.dt.int32
NEG = -1.0e30
BF = ml_dtypes.bfloat16
SIMMODE = os.environ.get("BASS_SIM_FP64") == "1"


class P:

    def __init__(self):
        self.nc = bass.Bass("TRN2", target_bir_lowering=False)
        self.out_names = []

    def inp(self, name, shp, dt=F32):
        return self.nc.dram_tensor(name, shp, dt, kind="ExternalInput").ap()

    def out(self, name, shp, dt=F32):
        self.out_names.append(name)
        return self.nc.dram_tensor(name, shp, dt, kind="ExternalOutput").ap()

    def __enter__(self):
        from contextlib import ExitStack
        self._es = ExitStack()
        nc = self.nc
        self.sb = lambda n, s, d=F32: self._es.enter_context(nc.sbuf_tensor(n, s, d))
        self.psum = lambda n, s: self._es.enter_context(nc.psum_tensor(n, s, F32))
        self.tk = self._es.enter_context(nc.semaphore("tok"))
        self.dmas = self._es.enter_context(nc.semaphore("dmas"))
        self.ntk = 0
        self.ndma = 0
        self.blk_cm = nc.Block()
        self.blk = self.blk_cm.__enter__()
        return self

    def step(self, eng, body, extra=()):
        n0 = self.ntk

        nd0 = self.ndma

        def sect(e):
            e.wait_ge(self.tk, n0)
            e.wait_ge(self.dmas, nd0)
            for s, v in extra:
                e.wait_ge(s, v)
            body(e)
            if eng != "sync":
                e.drain()
            e.sem_inc(self.tk, 1)
        getattr(self.blk, eng)(sect)
        self.ntk += 1

    def load(self, dst, src, slow=False):
        prior = [(self.dmas, self.ndma)]

        def body(e):
            e.dma_start(dst, src,
                        allow_slow_non_contiguous=slow).then_inc(self.dmas, 16)
        self.step("sync", body, extra=prior)
        self.ndma += 16
        return [(self.dmas, self.ndma)]

    def store(self, dst, src):
        return self.load(dst, src)

    def wd(self):
        return [(self.dmas, self.ndma)]

    def __exit__(self, *a):
        return False

    def finish(self):
        n0, nd = self.ntk, self.ndma

        def sect(e):
            e.wait_ge(self.tk, n0)
            e.wait_ge(self.dmas, nd)
        self.blk.sync(sect)
        self.blk_cm.__exit__(None, None, None)
        return self

    def run(self, in_maps):
        res = run_bass_kernel_spmd(self.nc, in_maps, core_ids=list(range(8)))
        return [{n: np.asarray(r[n]) for n in self.out_names} for r in res.results]


def recip_newton(e, dst, a, t1, t2):
    e.reciprocal(dst, a)
    e.drain()
    for _ in range(2):
        e.tensor_tensor(t1, a, dst, ALU.mult)
        e.drain()
        e.tensor_scalar(t2, t1, -1.0, 2.0, ALU.mult, ALU.add)
        e.drain()
        e.tensor_tensor(dst, dst, t2, ALU.mult)
        e.drain()


def quant_steps(p, e, src_f32, inv_ap, dst_bf, tmp_a, tmp_b, tmp_i):
    e.tensor_scalar(tmp_a, src_f32, inv_ap, None, ALU.mult)
    e.drain()
    if SIMMODE:
        e.tensor_scalar(tmp_b, tmp_a, 0.5, None, ALU.add)
        e.drain()
        e.tensor_scalar(tmp_a, tmp_b, 1.0, None, ALU.mod)
        e.drain()
        e.tensor_tensor(dst_bf, tmp_b, tmp_a, ALU.subtract)
        e.drain()
    else:
        e.tensor_copy(tmp_i, tmp_a)
        e.drain()
        e.tensor_copy(dst_bf, tmp_i)
        e.drain()


def build_ln(maxout=True):
    p = P()
    x = p.inp("x", [TS, D])
    xn = p.out("xn", [TS, D])
    mx = p.out("mx", [1, 1]) if maxout else None
    scr = p.nc.dram_tensor("scr", [128, 1], F32, kind="Internal").ap()
    with p:
        xt = p.sb("xt", [128, D])
        sq = p.sb("sq", [128, D])
        st = p.sb("st", [128, 8])
        acc = p.sb("acc", [128, 1])
        row = p.sb("row", [1, 128])
        p.step("vector", lambda e: (e.memset(acc[:], 0.0), e.drain()))
        for tc in range(8):
            w = p.load(xt[:], x[tc * 128:(tc + 1) * 128, :])
            p.step("scalar", lambda e: e.activation(sq[:], xt[:], AF.Square), extra=w)

            def stats(e):
                e.tensor_reduce(st[:, 0:1], xt[:], AX.X, ALU.add)
                e.tensor_reduce(st[:, 1:2], sq[:], AX.X, ALU.add)
                e.drain()
                e.tensor_scalar(st[:, 0:1], st[:, 0:1], 1.0 / D, None, ALU.mult)
                e.tensor_scalar(st[:, 1:2], st[:, 1:2], 1.0 / D, None, ALU.mult)
                e.drain()
                e.tensor_tensor(st[:, 2:3], st[:, 0:1], st[:, 0:1], ALU.mult)
                e.drain()
                e.tensor_tensor(st[:, 1:2], st[:, 1:2], st[:, 2:3], ALU.subtract)
                e.drain()
                e.tensor_scalar(st[:, 1:2], st[:, 1:2], 1e-5, None, ALU.add)
                e.drain()
            p.step("vector", stats)
            p.step("scalar", lambda e: e.activation(st[:, 3:4], st[:, 1:2], AF.Sqrt))

            def norm(e):
                e.reciprocal(st[:, 4:5], st[:, 3:4])
                e.drain()
                e.tensor_tensor(st[:, 4:5], st[:, 1:2], st[:, 4:5], ALU.mult)
                e.drain()
                e.tensor_tensor(st[:, 4:5], st[:, 4:5], st[:, 3:4], ALU.add)
                e.drain()
                e.tensor_scalar(st[:, 4:5], st[:, 4:5], 0.5, None, ALU.mult)
                e.drain()
                e.reciprocal(st[:, 4:5], st[:, 4:5])
                e.drain()
                e.tensor_scalar(xt[:], xt[:], st[:, 0:1], None, ALU.subtract)
                e.drain()
                e.tensor_scalar(xt[:], xt[:], st[:, 4:5], None, ALU.mult)
                e.drain()
                e.tensor_reduce(st[:, 5:6], xt[:], AX.X, ALU.max,
                                apply_absolute_value=True)
                e.drain()
                e.tensor_tensor(acc[:], acc[:], st[:, 5:6], ALU.max)
                e.drain()
            p.step("vector", norm)
            p.store(xn[tc * 128:(tc + 1) * 128, :], xt[:])
        if maxout:
            p.store(scr[:], acc[:])
            w = p.load(row[:], scr[:].rearrange("p a -> a p"), slow=True)
            p.step("vector", lambda e: (e.tensor_reduce(row[:, 0:1], row[:], AX.X,
                                                        ALU.max), e.drain()), extra=w)
            p.store(mx[:], row[0:1, 0:1])
        p.finish()
    return p




def scalar_from8(p, e, src8, dst, tmp, op):
    e.tensor_reduce(dst, src8, AX.X, op)
    e.drain()


def build_wsums():
    p = P()
    wgs = p.inp("wgs", [1024, D]); wus = p.inp("wus", [1024, D])
    wds = p.inp("wds", [256, IF])
    o = p.out("s3", [1, 3])
    with p:
        t = p.sb("t", [128, IF])
        c = p.sb("c", [128, 4])
        acc = p.sb("acc", [128, 3])
        row = p.sb("row", [1, 128])
        scr = p.nc.dram_tensor("scr", [128, 1], F32, kind="Internal").ap()
        p.step("vector", lambda e: e.memset(acc[:], 0.0))
        tiles = [(wgs[r * 128:(r + 1) * 128, :], 0) for r in range(8)] + \
                [(wus[r * 128:(r + 1) * 128, :], 1) for r in range(8)] + \
                [(wds[r * 128:(r + 1) * 128, c_ * 4096:(c_ + 1) * 4096], 2)
                 for r in range(2) for c_ in range(2)]
        for src, j in tiles:
            w = p.load(t[:, 0:src.shape[1]], src)

            def body(e, j=j, n=src.shape[1]):
                e.tensor_reduce(c[:, 0:1], t[:, 0:n], AX.X, ALU.add,
                                apply_absolute_value=True)
                e.drain()
                e.tensor_tensor(acc[:, j:j + 1], acc[:, j:j + 1], c[:, 0:1], ALU.add)
                e.drain()
            p.step("vector", body, extra=w)
        for j in range(3):
            p.store(scr[:], acc[:, j:j + 1])
            w = p.load(row[:], scr[:].rearrange("p a -> a p"), slow=True)
            p.step("vector", lambda e: (e.tensor_reduce(row[:, 0:1], row[:], AX.X,
                                                        ALU.add), e.drain()), extra=w)
            p.store(o[:, j:j + 1], row[0:1, 0:1])
        p.finish()
    return p


def build_wquant12():
    p = P()
    w1s = p.inp("w1s", [1024, D]); w2s = p.inp("w2s", [256, IF])
    qw1s = p.out("qw1s", [1024, D], BF16); qw2s = p.out("qw2s", [256, IF], BF16)
    s1 = p.out("s1", [1024, 1]); s2 = p.out("s2", [256, 1])
    with p:
        t = p.sb("t", [128, IF])
        q = p.sb("q", [128, IF], BF16)
        ta = p.sb("ta", [128, IF])
        tb = p.sb("tb", [128, IF])
        ti = p.sb("ti", [128, IF], I32)
        c = p.sb("c", [128, 4])
        for src, qdst, sdst, nr, ncol in [(w1s, qw1s, s1, 8, D), (w2s, qw2s, s2, 2, IF)]:
            for r in range(nr):
                w = p.load(t[:, 0:ncol], src[r * 128:(r + 1) * 128, :])

                def body(e, ncol=ncol):
                    e.tensor_reduce(c[:, 0:1], t[:, 0:ncol], AX.X, ALU.max,
                                    apply_absolute_value=True)
                    e.drain()
                    e.tensor_scalar(c[:, 0:1], c[:, 0:1], 1.0 / 127.0, 1e-8,
                                    ALU.mult, ALU.max)
                    e.drain()
                    recip_newton(e, c[:, 1:2], c[:, 0:1], c[:, 2:3], c[:, 3:4])
                    quant_steps(p, e, t[:, 0:ncol], c[:, 1:2], q[:, 0:ncol],
                                ta[:, 0:ncol], tb[:, 0:ncol], ti[:, 0:ncol])
                p.step("vector", body, extra=w)
                p.store(qdst[r * 128:(r + 1) * 128, :], q[:, 0:ncol])
                p.store(sdst[r * 128:(r + 1) * 128, :], c[:, 0:1])
        p.finish()
    return p


def build_wquantg():
    p = P()
    s3x8 = p.inp("s3x8", [8, 3])
    wgs = p.inp("wgs", [1024, D]); wus = p.inp("wus", [1024, D])
    wds = p.inp("wds", [256, IF])
    qg = p.out("qg", [1024, D], BF16); qu = p.out("qu", [1024, D], BF16)
    qd = p.out("qd", [256, IF], BF16)
    with p:
        t = p.sb("t", [128, IF])
        q = p.sb("q", [128, IF], BF16)
        tb = p.sb("tb", [128, IF])
        s8 = p.sb("s8", [8, 3])
        thr = p.sb("thr", [128, 3])
        c2 = p.sb("c2", [128, 1])
        row = p.sb("row", [1, 8])
        w = p.load(s8[:], s3x8[:])
        scr = p.nc.dram_tensor("scr", [8, 3], F32, kind="Internal").ap()
        p.store(scr[:], s8[:])
        thr_r = p.sb("thr_r", [1, 3])
        for j in range(3):
            wj = p.load(row[0:1, 0:8], scr[:, j:j + 1].rearrange("p a -> a p"), slow=True)

            def body(e, j=j):
                e.tensor_reduce(thr_r[:, j:j + 1], row[0:1, 0:8], AX.X, ALU.add)
                e.drain()
                e.tensor_scalar(thr_r[:, j:j + 1], thr_r[:, j:j + 1],
                                1.0 / (IF * D), 1e-5, ALU.mult, ALU.max)
                e.drain()
                e.tensor_scalar(thr_r[:, j:j + 1], thr_r[:, j:j + 1], 0.5,
                                None, ALU.mult)
                e.drain()
            p.step("vector", body, extra=wj)
        on1g = p.sb("on1g", [1, 128])
        psG = p.psum("psG", [128, 64])
        p.step("vector", lambda e: e.memset(on1g[:], 1.0))
        p.step("tensor", lambda e: e.matmul(psG[:, 0:3], on1g[:], thr_r[:, 0:3],
                                            start=True, stop=True))
        p.step("vector", lambda e: (e.tensor_copy(thr[:, 0:3], psG[:, 0:3]),
                                    e.drain()))
        for src, qdst, j, nr, ncol in [(wgs, qg, 0, 8, D), (wus, qu, 1, 8, D),
                                       (wds, qd, 2, 2, IF)]:
            for r in range(nr):
                w = p.load(t[:, 0:ncol], src[r * 128:(r + 1) * 128, :])

                def body(e, j=j, ncol=ncol):
                    e.tensor_scalar(q[:, 0:ncol], t[:, 0:ncol], thr[:, j:j + 1],
                                    None, ALU.is_ge)
                    e.drain()
                    e.tensor_scalar(thr[:, 3:4] if False else c2[:, 0:1],
                                    thr[:, j:j + 1], -1.0, None, ALU.mult)
                    e.drain()
                    e.tensor_scalar(tb[:, 0:ncol], t[:, 0:ncol], c2[:, 0:1],
                                    None, ALU.is_le)
                    e.drain()
                    e.tensor_tensor(q[:, 0:ncol], q[:, 0:ncol], tb[:, 0:ncol],
                                    ALU.subtract)
                    e.drain()
                p.step("vector", body, extra=w)
                p.store(qdst[r * 128:(r + 1) * 128, :], q[:, 0:ncol])
        p.finish()
    return p

def bcast_col(p, e_unused=None):
    pass


def build_ffn1():
    p = P()
    xnT = p.inp("xnT", [D, TS])
    qw1T = p.inp("qw1T", [D, IF], BF16)
    sw1fm = p.inp("sw1fm", [128, 64])
    b1fm = p.inp("b1fm", [128, 64])
    mxs = p.inp("mxs", [1, 8])
    h1 = p.out("h1", [IF, TS])
    hmax = p.out("hmax", [1, 1])
    with p:
        QX = p.sb("QX", [128, 16 * TS], BF16)
        xt = p.sb("xt", [128, TS])
        ta = p.sb("ta", [128, TS]); tb = p.sb("tb", [128, TS])
        ti = p.sb("ti", [128, TS], I32)
        wcol = p.sb("wcol", [128, 16 * 128], BF16)
        sw = p.sb("sw", [128, 64]); b1t = p.sb("b1t", [128, 64])
        c = p.sb("c", [128, 8])
        m8 = p.sb("m8", [1, 8])
        ht = p.sb("ht", [128, TS])
        acc = p.sb("acc", [128, 1])
        row = p.sb("row", [1, 128])
        psA = p.psum("psA", [128, 1024])
        psB = p.psum("psB", [128, 1024])
        scr = p.nc.dram_tensor("scr", [128, 1], F32, kind="Internal").ap()

        w = p.load(m8[:], mxs[:])
        w += p.load(sw[:], sw1fm[:])
        w += p.load(b1t[:], b1fm[:])

        def mkscale(e):
            e.tensor_reduce(m8[:, 0:1], m8[:], AX.X, ALU.max)
            e.drain()
            e.tensor_scalar(m8[:, 0:1], m8[:, 0:1], 1.0 / 127.0, 1e-8,
                            ALU.mult, ALU.max)
            e.drain()
            recip_newton(e, m8[:, 1:2], m8[:, 0:1], m8[:, 2:3], m8[:, 3:4])
        p.step("vector", mkscale, extra=w)

        on1g = p.sb("on1g", [1, 128])
        p.step("vector", lambda e: e.memset(on1g[:], 1.0))
        p.step("tensor", lambda e: (e.matmul(psA[:, 0:1], on1g[:], m8[:, 1:2],
                                             start=True, stop=True),
                                    e.matmul(psA[:, 1:2], on1g[:], m8[:, 0:1],
                                             start=True, stop=True)))
        p.step("vector", lambda e: (e.tensor_copy(c[:, 0:1], psA[:, 0:1]),
                                    e.drain(),
                                    e.tensor_copy(c[:, 1:2], psA[:, 1:2]),
                                    e.drain()))
        p.step("vector", lambda e: e.memset(acc[:], 0.0))

        for kc in range(16):
            w = p.load(xt[:], xnT[kc * 128:(kc + 1) * 128, :])
            p.step("vector",
                   lambda e, kc=kc: quant_steps(p, e, xt[:], c[:, 0:1],
                                                QX[:, kc * TS:(kc + 1) * TS],
                                                ta[:], tb[:], ti[:]), extra=w)
        for ic in range(64):
            w = p.load(wcol[:].rearrange("p (a c) -> p a c", a=16),
                       qw1T.rearrange("(a p) c -> p a c", p=128)
                       [:, :, ic * 128:(ic + 1) * 128])

            def mms(e):
                for k in range(16):
                    for th in range(2):
                        ps = psA if th == 0 else psB
                        e.matmul(ps[:, 0:512], wcol[:, k * 128:(k + 1) * 128],
                                 QX[:, k * TS + th * 512:k * TS + (th + 1) * 512],
                                 start=(k == 0), stop=(k == 15))
            p.step("tensor", mms, extra=w)

            def sc1(e, ic=ic):
                e.tensor_tensor(c[:, 2:3], sw[:, ic:ic + 1], c[:, 1:2], ALU.mult)
                e.drain()
            p.step("vector", sc1)
            p.step("scalar", lambda e, ic=ic: (
                e.activation(ht[:, 0:512], psA[:, 0:512], AF.Gelu,
                             bias=b1t[:, ic:ic + 1], scale=c[:, 2:3]),
                e.drain(),
                e.activation(ht[:, 512:1024], psB[:, 0:512], AF.Gelu,
                             bias=b1t[:, ic:ic + 1], scale=c[:, 2:3])))

            def mxacc(e):
                e.tensor_reduce(c[:, 3:4], ht[:], AX.X, ALU.max,
                                apply_absolute_value=True)
                e.drain()
                e.tensor_tensor(acc[:], acc[:], c[:, 3:4], ALU.max)
                e.drain()
            p.step("vector", mxacc)
            p.store(h1[ic * 128:(ic + 1) * 128, :], ht[:])
        p.store(scr[:], acc[:])
        w = p.load(row[:], scr[:].rearrange("p a -> a p"), slow=True)
        p.step("vector", lambda e: (e.tensor_reduce(row[:, 0:1], row[:], AX.X,
                                                    ALU.max), e.drain()), extra=w)
        p.store(hmax[:], row[0:1, 0:1])
        p.finish()
    return p


def build_ffn2():
    p = P()
    h1 = p.inp("h1", [IF, TS])
    qw2T = p.inp("qw2T", [IF, D], BF16)
    sw2fm = p.inp("sw2fm", [128, 16])
    b2fm = p.inp("b2fm", [128, 16])
    hmaxs = p.inp("hmaxs", [1, 8])
    o2 = p.out("o2", [D, TS])
    with p:
        QH = p.sb("QH", [128, 8 * TS], BF16)
        ht = p.sb("ht", [128, TS])
        ta = p.sb("ta", [128, TS]); tb = p.sb("tb", [128, TS])
        ti = p.sb("ti", [128, TS], I32)
        wcol = p.sb("wcol", [128, 8 * 128], BF16)
        sw = p.sb("sw", [128, 16]); b2t = p.sb("b2t", [128, 16])
        OACC = p.sb("OACC", [128, 16 * TS])
        c = p.sb("c", [128, 8])
        m8 = p.sb("m8", [1, 8])
        psA = p.psum("psA", [128, 1024])
        w = p.load(m8[:], hmaxs[:])
        w += p.load(sw[:], sw2fm[:])
        w += p.load(b2t[:], b2fm[:])

        def mkscale(e):
            e.tensor_reduce(m8[:, 0:1], m8[:], AX.X, ALU.max)
            e.drain()
            e.tensor_scalar(m8[:, 0:1], m8[:, 0:1], 1.0 / 127.0, 1e-8,
                            ALU.mult, ALU.max)
            e.drain()
            recip_newton(e, m8[:, 1:2], m8[:, 0:1], m8[:, 2:3], m8[:, 3:4])
        p.step("vector", mkscale, extra=w)

        on1g = p.sb("on1g", [1, 128])
        p.step("vector", lambda e: e.memset(on1g[:], 1.0))
        p.step("tensor", lambda e: (e.matmul(psA[:, 0:1], on1g[:], m8[:, 1:2],
                                             start=True, stop=True),
                                    e.matmul(psA[:, 1:2], on1g[:], m8[:, 0:1],
                                             start=True, stop=True)))
        p.step("vector", lambda e: (e.tensor_copy(c[:, 0:1], psA[:, 0:1]),
                                    e.drain(),
                                    e.tensor_copy(c[:, 1:2], psA[:, 1:2]),
                                    e.drain()))

        def sc2(e):
            e.tensor_scalar(sw[:], sw[:], c[:, 1:2], None, ALU.mult)
            e.drain()
        p.step("vector", sc2)
        p.step("vector", lambda e: e.memset(OACC[:], 0.0))

        for ig in range(8):
            for j in range(8):
                icg = ig * 8 + j
                w = p.load(ht[:], h1[icg * 128:(icg + 1) * 128, :])
                p.step("vector",
                       lambda e, j=j: quant_steps(p, e, ht[:], c[:, 0:1],
                                                  QH[:, j * TS:(j + 1) * TS],
                                                  ta[:], tb[:], ti[:]), extra=w)
            for oc in range(16):
                w = p.load(wcol[:].rearrange("p (a c) -> p a c", a=8),
                           qw2T.rearrange("(a p) c -> p a c", p=128)
                           [:, ig * 8:(ig + 1) * 8, oc * 128:(oc + 1) * 128])

                def mms(e):
                    for k in range(8):
                        for th in range(2):
                            e.matmul(psA[:, th * 512:(th + 1) * 512],
                                     wcol[:, k * 128:(k + 1) * 128],
                                     QH[:, k * TS + th * 512:k * TS + (th + 1) * 512],
                                     start=(k == 0), stop=(k == 7))
                p.step("tensor", mms, extra=w)

                def evac(e, oc=oc):
                    e.scalar_tensor_tensor(OACC[:, oc * TS:(oc + 1) * TS], psA[:],
                                           sw[:, oc:oc + 1],
                                           OACC[:, oc * TS:(oc + 1) * TS],
                                           ALU.mult, ALU.add)
                    e.drain()
                p.step("vector", evac)
        for oc in range(16):
            def fin(e, oc=oc):
                e.tensor_scalar(OACC[:, oc * TS:(oc + 1) * TS],
                                OACC[:, oc * TS:(oc + 1) * TS],
                                b2t[:, oc:oc + 1], None, ALU.add)
                e.drain()
            p.step("vector", fin)
            p.store(o2[oc * 128:(oc + 1) * 128, :], OACC[:, oc * TS:(oc + 1) * TS])
        p.finish()
    return p

def build_xdbl():
    p = P()
    xnT = p.inp("xnT", [D, TS])
    xpwT = p.inp("xpwT", [D, 96])
    xd = p.out("xd", [96, TS])
    with p:
        xt = p.sb("xt", [128, TS])
        wt = p.sb("wt", [128, 16 * 96])
        o = p.sb("o", [96, TS])
        psA = p.psum("psA", [128, 1024])
        w = p.load(wt[:].rearrange("p (a c) -> p a c", a=16),
                   xpwT.rearrange("(a p) c -> p a c", p=128))
        d = None
        for kc in range(16):
            wx = p.load(xt[:], xnT[kc * 128:(kc + 1) * 128, :])

            def mms(e, kc=kc):
                for th in range(2):
                    e.matmul(psA[0:96, th * 512:(th + 1) * 512],
                             wt[:, kc * 96:(kc + 1) * 96],
                             xt[:, th * 512:(th + 1) * 512],
                             start=(kc == 0), stop=(kc == 15))
            p.step("tensor", mms, extra=w + wx)
        p.step("vector", lambda e: e.tensor_copy(o[:], psA[0:96, :]))
        p.store(xd[:], o[:])
        p.finish()
    return p


def build_mamba():
    p = P()
    L = 2048
    xnS = p.inp("xnS", [256, 4 * L])
    dtwTS = p.inp("dtwTS", [64, 256])
    dtbS = p.inp("dtbS", [128, 2])
    alogS = p.inp("alogS", [128, 32])
    dS = p.inp("dS", [128, 2])
    xdd = p.inp("xdd", [64, 4 * L])
    xdb = p.inp("xdb", [16, 4 * L])
    xdc = p.inp("xdc", [16, 4 * L])
    y = p.out("y", [256, 4 * L])
    with p:
        XNdc = p.sb("XNdc", [128, 4 * L])
        Ydc = p.sb("Ydc", [128, 4 * L])
        dtw = p.sb("dtw", [64, 256])
        dtb = p.sb("dtb", [128, 2])
        alog = p.sb("alog", [128, 32])
        nega = p.sb("nega", [128, 32])
        dvec = p.sb("dvec", [128, 2])
        xddt = p.sb("xddt", [64, 4 * L])
        xdbt = p.sb("xdbt", [1, L])
        xdct = p.sb("xdct", [1, L])
        delta = p.sb("delta", [128, 4 * L])
        brep = p.sb("brep", [128, L])
        crep = p.sb("crep", [128, L])
        dA = p.sb("dA", [128, L], BF16)
        dBu = p.sb("dBu", [128, L])
        hh = p.sb("hh", [128, L])
        hc = p.sb("hc", [128, L])
        one1 = p.sb("one1", [1, 128])
        psA = p.psum("psA", [128, 1024])
        psB = p.psum("psB", [128, 1024])
        w0 = p.load(dtw[:], dtwTS[:])
        w0 += p.load(dtb[:], dtbS[:])
        w0 += p.load(alog[:], alogS[:])
        w0 += p.load(dvec[:], dS[:])
        w0 += p.load(xddt[:], xdd[:])
        p.step("scalar", lambda e: e.activation(nega[:], alog[:], AF.Exp), extra=w0)
        p.step("vector", lambda e: (e.tensor_scalar(nega[:], nega[:], -1.0, None,
                                                    ALU.mult), e.drain()))
        p.step("vector", lambda e: e.memset(one1[:], 1.0))
        for dc in range(2):
            wx = p.load(XNdc[:], xnS[dc * 128:(dc + 1) * 128, :])
            def dmm(e, dc=dc):
                for seg in range(8):
                    e.matmul(psA[:, 0:512] if seg % 2 == 0 else psA[:, 512:1024],
                             dtw[:, dc * 128:(dc + 1) * 128],
                             xddt[:, seg * 1024:seg * 1024 + 512],
                             start=True, stop=True)
                    e.matmul(psB[:, 0:512] if seg % 2 == 0 else psB[:, 512:1024],
                             dtw[:, dc * 128:(dc + 1) * 128],
                             xddt[:, seg * 1024 + 512:(seg + 1) * 1024],
                             start=True, stop=True)
            for seg in range(8):
                def mm2(e, dc=dc, seg=seg):
                    e.matmul(psA[:, 0:512], dtw[:, dc * 128:(dc + 1) * 128],
                             xddt[:, seg * 1024:seg * 1024 + 512],
                             start=True, stop=True)
                    e.matmul(psA[:, 512:1024], dtw[:, dc * 128:(dc + 1) * 128],
                             xddt[:, seg * 1024 + 512:(seg + 1) * 1024],
                             start=True, stop=True)
                p.step("tensor", mm2, extra=(wx if seg == 0 else ()))
                p.step("scalar", lambda e, dc=dc, seg=seg: (
                    e.activation(delta[:, seg * 1024:(seg + 1) * 1024], psA[:],
                                 AF.Exp, bias=dtb[:, dc:dc + 1]),
                    e.drain(),
                    e.activation(delta[:, seg * 1024:(seg + 1) * 1024],
                                 delta[:, seg * 1024:(seg + 1) * 1024],
                                 AF.Ln, bias=1.0)))
            p.step("vector", lambda e, dc=dc: (
                e.tensor_scalar(Ydc[:], XNdc[:], dvec[:, dc:dc + 1], None,
                                ALU.mult), e.drain()))
            for b in range(4):
                for n in range(16):
                    wb_ = p.load(xdbt[:], xdb[n:n + 1, b * L:(b + 1) * L])
                    wb_ += p.load(xdct[:], xdc[n:n + 1, b * L:(b + 1) * L])
                    p.step("vector", lambda e: None, extra=wb_)
                    def reps(e, b=b, n=n):
                        for th in range(4):
                            e.matmul(psA[:, th * 512:(th + 1) * 512] if th < 2 else
                                     psB[:, (th - 2) * 512:(th - 1) * 512],
                                     one1[:],
                                     (xdbt if th < 2 else xdct)
                                     [n:n + 1, b * L + (th % 2) * 1024:b * L + (th % 2) * 1024 + 512]
                                     if False else None,
                                     start=True, stop=True) if False else None
                        e.matmul(psA[:, 0:512], one1[:],
                                 xdbt[0:1, 0:512], start=True, stop=True)
                        e.matmul(psA[:, 512:1024], one1[:],
                                 xdbt[0:1, 512:1024], start=True, stop=True)
                        e.matmul(psB[:, 0:512], one1[:],
                                 xdct[0:1, 0:512], start=True, stop=True)
                        e.matmul(psB[:, 512:1024], one1[:],
                                 xdct[0:1, 512:1024], start=True, stop=True)
                    p.step("tensor", reps)
                    for hl in range(2):
                        if hl == 1:
                            def reps2(e, b=b, n=n):
                                e.matmul(psA[:, 0:512], one1[:],
                                         xdbt[0:1, 1024:1536],
                                         start=True, stop=True)
                                e.matmul(psA[:, 512:1024], one1[:],
                                         xdbt[0:1, 1536:2048],
                                         start=True, stop=True)
                                e.matmul(psB[:, 0:512], one1[:],
                                         xdct[0:1, 1024:1536],
                                         start=True, stop=True)
                                e.matmul(psB[:, 512:1024], one1[:],
                                         xdct[0:1, 1536:2048],
                                         start=True, stop=True)
                            p.step("tensor", reps2)

                        def evrep(e, hl=hl):
                            e.tensor_copy(brep[:, hl * 1024:(hl + 1) * 1024], psA[:])
                            e.drain()
                            e.tensor_copy(crep[:, hl * 1024:(hl + 1) * 1024], psB[:])
                            e.drain()
                        p.step("vector", evrep)
                    p.step("scalar", lambda e, b=b, n=n, dc=dc: e.activation(
                        dA[:], delta[:, b * L:(b + 1) * L], AF.Exp,
                        scale=nega[:, dc * 16 + n:dc * 16 + n + 1]))

                    def scanops(e, b=b, n=n):
                        e.tensor_tensor(dBu[:], delta[:, b * L:(b + 1) * L],
                                        brep[:], ALU.mult)
                        e.drain()
                        e.tensor_tensor(dBu[:], dBu[:],
                                        XNdc[:, b * L:(b + 1) * L], ALU.mult)
                        e.drain()
                        e.tensor_tensor_scan(hh[:], dA[:], dBu[:], 0.0,
                                             ALU.mult, ALU.add)
                        e.drain()
                        e.tensor_tensor(hc[:], hh[:], crep[:], ALU.mult)
                        e.drain()
                        e.tensor_tensor(Ydc[:, b * L:(b + 1) * L],
                                        Ydc[:, b * L:(b + 1) * L], hc[:], ALU.add)
                        e.drain()
                    p.step("vector", scanops)
            p.store(y[dc * 128:(dc + 1) * 128, :], Ydc[:])
        p.finish()
    return p

def build_resid_ln():
    p = P()
    x = p.inp("x", [TS, D]); yt = p.inp("yt", [TS, D]); o2t = p.inp("o2t", [TS, D])
    x1 = p.out("x1", [TS, D]); xn1 = p.out("xn1", [TS, D])
    qxn1 = p.out("qxn1", [TS, D], BF16)
    fs = p.out("fs", [TS, 1]); fcl = p.out("fcl", [TS, 1])
    with p:
        xt = p.sb("xt", [128, D]); ot = p.sb("ot", [128, D])
        sq = p.sb("sq", [128, D])
        qt = p.sb("qt", [128, D], BF16)
        ta = p.sb("ta", [128, D]); tb = p.sb("tb", [128, D])
        ti = p.sb("ti", [128, D], I32)
        st = p.sb("st", [128, 12])
        for tc in range(8):
            w = p.load(xt[:], x[tc * 128:(tc + 1) * 128, :])
            w += p.load(ot[:], yt[tc * 128:(tc + 1) * 128, :])
            p.step("vector", lambda e: (e.tensor_tensor(xt[:], xt[:], ot[:],
                                                        ALU.add), e.drain()), extra=w)
            w = p.load(ot[:], o2t[tc * 128:(tc + 1) * 128, :])
            p.step("vector", lambda e: (e.tensor_tensor(xt[:], xt[:], ot[:],
                                                        ALU.add), e.drain()), extra=w)
            p.store(x1[tc * 128:(tc + 1) * 128, :], xt[:])
            p.step("scalar", lambda e: e.activation(sq[:], xt[:], AF.Square))

            def stats(e):
                e.tensor_reduce(st[:, 0:1], xt[:], AX.X, ALU.add)
                e.tensor_reduce(st[:, 1:2], sq[:], AX.X, ALU.add)
                e.drain()
                e.tensor_scalar(st[:, 0:1], st[:, 0:1], 1.0 / D, None, ALU.mult)
                e.tensor_scalar(st[:, 1:2], st[:, 1:2], 1.0 / D, None, ALU.mult)
                e.drain()
                e.tensor_tensor(st[:, 2:3], st[:, 0:1], st[:, 0:1], ALU.mult)
                e.drain()
                e.tensor_tensor(st[:, 1:2], st[:, 1:2], st[:, 2:3], ALU.subtract)
                e.drain()
                e.tensor_scalar(st[:, 1:2], st[:, 1:2], 1e-5, None, ALU.add)
                e.drain()
            p.step("vector", stats)
            p.step("scalar", lambda e: e.activation(st[:, 3:4], st[:, 1:2], AF.Sqrt))

            def norm(e):
                e.reciprocal(st[:, 4:5], st[:, 3:4])
                e.drain()
                e.tensor_tensor(st[:, 4:5], st[:, 1:2], st[:, 4:5], ALU.mult)
                e.drain()
                e.tensor_tensor(st[:, 4:5], st[:, 4:5], st[:, 3:4], ALU.add)
                e.drain()
                e.tensor_scalar(st[:, 4:5], st[:, 4:5], 0.5, None, ALU.mult)
                e.drain()
                e.reciprocal(st[:, 4:5], st[:, 4:5])
                e.drain()
                e.tensor_scalar(xt[:], xt[:], st[:, 0:1], None, ALU.subtract)
                e.drain()
                e.tensor_scalar(xt[:], xt[:], st[:, 4:5], None, ALU.mult)
                e.drain()
                e.tensor_reduce(st[:, 5:6], xt[:], AX.X, ALU.max,
                                apply_absolute_value=True)
                e.drain()
                e.tensor_scalar(st[:, 5:6], st[:, 5:6], 1e-5, None, ALU.max)
                e.drain()
                recip_newton(e, st[:, 6:7], st[:, 5:6], st[:, 7:8], st[:, 8:9])
                e.tensor_scalar(st[:, 6:7], st[:, 6:7], 127.0, None, ALU.mult)
                e.drain()
                quant_steps(p, e, xt[:], st[:, 6:7], qt[:], ta[:], tb[:], ti[:])
            p.step("vector", norm)
            p.store(xn1[tc * 128:(tc + 1) * 128, :], xt[:])
            p.store(qxn1[tc * 128:(tc + 1) * 128, :], qt[:])
            p.store(fs[tc * 128:(tc + 1) * 128, :], st[:, 6:7])
            p.store(fcl[tc * 128:(tc + 1) * 128, :], st[:, 5:6])
        p.finish()
    return p


def build_qkv():
    p = P()
    xn1T = p.inp("xn1T", [D, TS])
    wqkv = p.inp("wqkv", [D, 3 * D])
    bqkvrow = p.inp("bqkvrow", [1, 3 * D])
    wgate = p.inp("wgate", [D, 32])
    bgrow = p.inp("bgrow", [1, 32])
    ctabq = p.inp("ctabq", [128, 8 * 32]); stabq = p.inp("stabq", [128, 8 * 32])
    ctabk = p.inp("ctabk", [128, 8 * 32]); stabk = p.inp("stabk", [128, 8 * 32])
    q_tm = p.out("q_tm", [TS, D], BF16)
    k_tm = p.out("k_tm", [TS, D], BF16)
    v_tm = p.out("v_tm", [TS, D], BF16)
    g1m = p.out("g1m", [TS, 32]); gm = p.out("gm", [TS, 32])
    with p:
        BX = p.sb("BX", [128, 16 * TS], BF16)
        xt = p.sb("xt", [128, TS])
        wf = p.sb("wf", [128, 16 * 512])
        wb = p.sb("wb", [128, 16 * 512], BF16)
        br = p.sb("br", [128, 512])
        ct = p.sb("ct", [128, 8 * 32]); stt = p.sb("stt", [128, 8 * 32])
        ctk = p.sb("ctk", [128, 8 * 32]); stk = p.sb("stk", [128, 8 * 32])
        t0 = p.sb("t0", [128, 512])
        r0 = p.sb("r0", [128, 256]); r1 = p.sb("r1", [128, 256])
        oev = p.sb("oev", [128, 256]); ood = p.sb("ood", [128, 256])
        qb = p.sb("qb", [128, 512], BF16)
        gw = p.sb("gw", [128, 16 * 32])
        gwb = p.sb("gwb", [128, 16 * 32], BF16)
        gbr = p.sb("gbr", [128, 32])
        gt = p.sb("gt", [128, 32])
        p.ones_sb = p.sb("ones_sb", [1, 128])
        p.bqrow_sb = p.sb("bqrow_sb", [1, 3 * D])
        p.bgrow_sb = p.sb("bgrow_sb", [1, 32])
        psA = p.psum("psA", [128, 1024])
        psB = p.psum("psB", [128, 1024])
        p.step("vector", lambda e: e.memset(p.ones_sb[:], 1.0))
        w0 = p.load(p.bqrow_sb[:], bqkvrow[:])
        w0 += p.load(p.bgrow_sb[:], bgrow[:])
        w0 += p.load(ct[:], ctabq[:])
        w0 += p.load(stt[:], stabq[:])
        w0 += p.load(ctk[:], ctabk[:])
        w0 += p.load(stk[:], stabk[:])
        for kc in range(16):
            w = p.load(xt[:], xn1T[kc * 128:(kc + 1) * 128, :])
            p.step("vector", lambda e, kc=kc: (
                e.tensor_copy(BX[:, kc * TS:(kc + 1) * TS], xt[:]), e.drain()),
                extra=w + (w0 if kc == 0 else []))
        for ocol in range(12):
            w = p.load(wf[:].rearrange("p (a c) -> p a c", a=16),
                       wqkv.rearrange("(a p) c -> p a c", p=128)
                       [:, :, ocol * 512:(ocol + 1) * 512])
            p.step("vector", lambda e: (e.tensor_copy(wb[:], wf[:]), e.drain()),
                   extra=w)
            p.step("tensor", lambda e, ocol=ocol: e.matmul(
                psB[:, 0:512], p.ones_sb[:],
                p.bqrow_sb[:, ocol * 512:(ocol + 1) * 512], start=True, stop=True))
            p.step("vector", lambda e: (e.tensor_copy(br[:], psB[:, 0:512]),
                                        e.drain()))
            for tc in range(8):
                def mms(e, tc=tc):
                    for k in range(16):
                        e.matmul(psA[:, 0:512], BX[:, k * TS + tc * 128:
                                                   k * TS + (tc + 1) * 128],
                                 wb[:, k * 512:(k + 1) * 512],
                                 start=(k == 0), stop=(k == 15))
                p.step("tensor", mms)

                def evac(e, ocol=ocol, tc=tc):
                    e.tensor_tensor(t0[:], psA[:, 0:512], br[:], ALU.add)
                    e.drain()
                p.step("vector", evac)
                if ocol >= 8:
                    p.step("vector", lambda e: (e.tensor_copy(qb[:], t0[:]),
                                                e.drain()))
                else:
                    isq = ocol < 4
                    ctu, stu = (ct, stt) if isq else (ctk, stk)

                    def rope(e, tc=tc, ctu=ctu, stu=stu):
                        tv = t0[:].rearrange("p (a j two) -> p (a j) two", two=2, j=32)
                        ev = tv[:, :, 0:1].rearrange("p a b -> p (a b)") \
                            .rearrange("p (a j) -> p a j", j=32)
                        od = tv[:, :, 1:2].rearrange("p a b -> p (a b)") \
                            .rearrange("p (a j) -> p a j", j=32)
                        cb = ctu[:, tc * 32:(tc + 1) * 32].rearrange(
                            "p (o j) -> p o j", o=1)
                        sbt = stu[:, tc * 32:(tc + 1) * 32].rearrange(
                            "p (o j) -> p o j", o=1)
                        r03 = r0[:].rearrange("p (a j) -> p a j", j=32)
                        r13 = r1[:].rearrange("p (a j) -> p a j", j=32)
                        _, cbx = bass.broadcast_tensor_aps(r03, cb)
                        _, sbx = bass.broadcast_tensor_aps(r03, sbt)
                        oev3 = oev[:].rearrange("p (a j) -> p a j", j=32)
                        ood3 = ood[:].rearrange("p (a j) -> p a j", j=32)
                        e.tensor_tensor(r03, ev, cbx, ALU.mult)
                        e.drain()
                        e.tensor_tensor(r13, od, sbx, ALU.mult)
                        e.drain()
                        e.tensor_tensor(oev3, r03, r13, ALU.subtract)
                        e.drain()
                        e.tensor_tensor(r03, ev, sbx, ALU.mult)
                        e.drain()
                        e.tensor_tensor(r13, od, cbx, ALU.mult)
                        e.drain()
                        e.tensor_tensor(ood3, r03, r13, ALU.add)
                        e.drain()
                        q3 = qb[:].rearrange("p (a j two) -> p (a j) two", two=2, j=32)
                        qe = q3[:, :, 0:1].rearrange("p a b -> p (a b)") \
                            .rearrange("p (a j) -> p a j", j=32)
                        qo = q3[:, :, 1:2].rearrange("p a b -> p (a b)") \
                            .rearrange("p (a j) -> p a j", j=32)
                        e.tensor_copy(qe, oev3)
                        e.drain()
                        e.tensor_copy(qo, ood3)
                        e.drain()
                    p.step("vector", rope)
                dstm = v_tm if ocol >= 8 else (q_tm if ocol < 4 else k_tm)
                oc2 = ocol - 8 if ocol >= 8 else (ocol if ocol < 4 else ocol - 4)
                p.store(dstm[tc * 128:(tc + 1) * 128, oc2 * 512:(oc2 + 1) * 512],
                        qb[:])
        w = p.load(gw[:].rearrange("p (a c) -> p a c", a=16),
                   wgate.rearrange("(a p) c -> p a c", p=128))
        p.step("vector", lambda e: (e.tensor_copy(gwb[:], gw[:]), e.drain()), extra=w)
        p.step("tensor", lambda e: e.matmul(psB[:, 0:32], p.ones_sb[:],
                                             p.bgrow_sb[:], start=True, stop=True))
        p.step("vector", lambda e: (e.tensor_copy(gbr[:], psB[:, 0:32]), e.drain()))
        for tc in range(8):
            def gmm(e, tc=tc):
                for k in range(16):
                    e.matmul(psA[:, 0:32], BX[:, k * TS + tc * 128:
                                              k * TS + (tc + 1) * 128],
                             gwb[:, k * 32:(k + 1) * 32],
                             start=(k == 0), stop=(k == 15))
            p.step("tensor", gmm)
            p.step("vector", lambda e: (e.tensor_tensor(gt[:], psA[:, 0:32],
                                                        gbr[:], ALU.add), e.drain()))
            p.step("scalar", lambda e: e.activation(gt[:], gt[:], AF.Sigmoid))
            p.store(gm[tc * 128:(tc + 1) * 128, :], gt[:])
            p.step("vector", lambda e: (e.tensor_scalar(gt[:], gt[:], -1.0, 1.0,
                                                        ALU.mult, ALU.add),
                                        e.drain()))
            p.store(g1m[tc * 128:(tc + 1) * 128, :], gt[:])
        p.finish()
    return p



def build_locpool():
    p = P()
    q_fm = p.inp("q_fm", [D, TS], BF16)
    k_fm = p.inp("k_fm", [D, TS], BF16)
    v_tm = p.inp("v_tm", [TS, D], BF16)
    pqT = p.inp("pqT", [D, 4])
    g1l = p.inp("g1l", [64, 16 * 32])
    maskl = p.inp("maskl", [64, 64])
    identb = p.inp("identb", [128, 128], BF16)
    olocal = p.out("olocal", [D, TS], BF16)
    lmf = p.out("lmf", [D, 64], BF16)
    with p:
        qh = p.sb("qh", [64, TS], BF16)
        kh = p.sb("kh", [64, TS], BF16)
        vh = p.sb("vh", [64, 16 * 64], BF16)
        pqh = p.sb("pqh", [64, 8])
        pqb = p.sb("pqb", [64, 4], BF16)
        g1h = p.sb("g1h", [64, 16])
        mlk = p.sb("mlk", [64, 64])
        idb = p.sb("idb", [128, 128], BF16)
        sc_ = p.sb("sc_", [64, TS])
        st = p.sb("st", [64, 48])
        pb = p.sb("pb", [64, TS], BF16)
        ptb = p.sb("ptb", [64, TS], BF16)
        ob = p.sb("ob", [64, TS], BF16)
        pp = p.sb("pp", [64, TS])
        pab = p.sb("pab", [64, 64], BF16)
        patb = p.sb("patb", [64, 64], BF16)
        lmh = p.sb("lmh", [64, 64], BF16)
        psA = p.psum("psA", [128, 1024])
        psB = p.psum("psB", [128, 1024])
        w0 = p.load(mlk[:], maskl[:])
        w0 += p.load(idb[:], identb[:])
        p.step("vector", lambda e: None, extra=w0)
        v3 = v_tm.rearrange("(blk s) d -> s blk d", s=64)
        for h in range(32):
            w = p.load(qh[:], q_fm[h * 64:(h + 1) * 64, :])
            w += p.load(kh[:], k_fm[h * 64:(h + 1) * 64, :])
            w += p.load(vh[:].rearrange("p (a c) -> p a c", a=16),
                        v3[:, :, h * 64:(h + 1) * 64])
            w += p.load(pqh[:, 0:4], pqT[h * 64:(h + 1) * 64, :])
            w += p.load(g1h[:].rearrange("p (b o) -> p b o", o=1),
                        g1l.rearrange("p (b g) -> p b g", g=32)[:, :, h:h + 1],
                        slow=True)
            p.step("vector", lambda e: (e.tensor_scalar(pqh[:, 4:8], pqh[:, 0:4],
                                                        0.125, None, ALU.mult),
                                        e.drain(),
                                        e.tensor_copy(pqb[:], pqh[:, 4:8]),
                                        e.drain()), extra=w)

            def smm(e):
                for blk in range(16):
                    e.matmul(psA[0:64, blk * 64:(blk + 1) * 64],
                             qh[:, blk * 64:(blk + 1) * 64],
                             kh[:, blk * 64:(blk + 1) * 64], start=True, stop=True)
                    e.matmul(psB[0:4, blk * 64:(blk + 1) * 64],
                             pqb[:],
                             kh[:, blk * 64:(blk + 1) * 64], start=True, stop=True)
            p.step("tensor", smm)

            def lsm(e):
                s3 = sc_[:].rearrange("p (b c) -> p b c", b=16)
                m3 = mlk[:].rearrange("p (o c) -> p o c", o=1)
                ps3 = psA[0:64, :].rearrange("p (b c) -> p b c", b=16)
                _, m3b = bass.broadcast_tensor_aps(ps3, m3)
                e.tensor_tensor(s3, ps3, m3b, ALU.add)
                e.drain()
                mx = st[:, 0:16].rearrange("p (b c) -> p b c", c=1)
                e.tensor_reduce(mx, s3, AX.X, ALU.max)
                e.drain()
                _, mxb = bass.broadcast_tensor_aps(s3, mx)
                e.tensor_tensor(s3, s3, mxb, ALU.subtract)
                e.drain()
            p.step("vector", lsm)
            p.step("scalar", lambda e: e.activation(sc_[:], sc_[:], AF.Exp))

            def lnorm(e):
                s3 = sc_[:].rearrange("p (b c) -> p b c", b=16)
                sm = st[:, 16:32].rearrange("p (b c) -> p b c", c=1)
                e.tensor_reduce(sm, s3, AX.X, ALU.add)
                e.drain()
                e.reciprocal(st[:, 16:32], st[:, 16:32])
                e.drain()
                e.tensor_tensor(st[:, 16:32], st[:, 16:32], g1h[:], ALU.mult)
                e.drain()
                sm2 = st[:, 16:32].rearrange("p (b c) -> p b c", c=1)
                _, smb = bass.broadcast_tensor_aps(s3, sm2)
                p3 = pb[:].rearrange("p (b c) -> p b c", b=16)
                e.tensor_tensor(p3, s3, smb, ALU.mult)
                e.drain()
            p.step("vector", lnorm)

            def ptr(e):
                for blk in range(16):
                    e.matmul(psA[0:64, blk * 64:(blk + 1) * 64],
                             pb[:, blk * 64:(blk + 1) * 64],
                             idb[0:64, 0:64], start=True, stop=True)
            p.step("tensor", ptr)
            p.step("vector", lambda e: (e.tensor_copy(ptb[:], psA[0:64, :]),
                                        e.drain()))

            def omm(e):
                for blk in range(16):
                    e.matmul(psB[0:64, blk * 64:(blk + 1) * 64],
                             vh[:, blk * 64:(blk + 1) * 64],
                             ptb[:, blk * 64:(blk + 1) * 64], start=True, stop=True)
            p.step("tensor", omm)
            p.step("vector", lambda e: (e.tensor_copy(ob[:], psB[0:64, :]),
                                        e.drain()))
            p.store(olocal[h * 64:(h + 1) * 64, :], ob[:])

            def psm0(e):
                for blk in range(16):
                    e.matmul(psA[0:4, blk * 64:(blk + 1) * 64],
                             pqb[:],
                             kh[:, blk * 64:(blk + 1) * 64], start=True, stop=True)
            p.step("tensor", psm0)

            def psm(e):
                e.tensor_copy(pp[0:4, :], psA[0:4, :])
                e.drain()
                s3 = pp[0:4, :].rearrange("p (b c) -> p b c", b=16)
                mx = st[0:4, 32:48].rearrange("p (b c) -> p b c", c=1)
                e.tensor_reduce(mx, s3, AX.X, ALU.max)
                e.drain()
                _, mxb = bass.broadcast_tensor_aps(s3, mx)
                e.tensor_tensor(s3, s3, mxb, ALU.subtract)
                e.drain()
            p.step("vector", psm)
            p.step("scalar", lambda e: e.activation(pp[0:4, :], pp[0:4, :], AF.Exp))

            def pnorm(e):
                s3 = pp[0:4, :].rearrange("p (b c) -> p b c", b=16)
                sm = st[0:4, 32:48].rearrange("p (b c) -> p b c", c=1)
                e.tensor_reduce(sm, s3, AX.X, ALU.add)
                e.drain()
                e.reciprocal(st[0:4, 32:48], st[0:4, 32:48])
                e.drain()
                sm2 = st[0:4, 32:48].rearrange("p (b c) -> p b c", c=1)
                _, smb = bass.broadcast_tensor_aps(s3, sm2)
                p3 = pp[0:4, :].rearrange("p (b c) -> p b c", b=16)
                e.tensor_tensor(p3, s3, smb, ALU.mult)
                e.drain()
                e.tensor_copy(pab[0:4, :] if False else pb[0:4, :], pp[0:4, :])
                e.drain()
            p.step("vector", pnorm)

            def patr(e):
                for blk in range(16):
                    e.matmul(psA[0:64, blk * 4:(blk + 1) * 4],
                             pb[0:4, blk * 64:(blk + 1) * 64],
                             idb[0:4, 0:4], start=True, stop=True)
            p.step("tensor", patr)
            p.step("vector", lambda e: (e.tensor_copy(patb[:, 0:64], psA[0:64, 0:64]),
                                        e.drain()))

            def lmm(e):
                for blk in range(16):
                    e.matmul(psB[0:64, blk * 4:(blk + 1) * 4],
                             vh[:, blk * 64:(blk + 1) * 64],
                             patb[:, blk * 4:(blk + 1) * 4], start=True, stop=True)
            p.step("tensor", lmm)
            p.step("vector", lambda e: (e.tensor_copy(lmh[:, 0:64], psB[0:64, 0:64]),
                                        e.drain()))
            p.store(lmf[h * 64:(h + 1) * 64, :], lmh[:, 0:64])
        p.finish()
    return p

def build_latent():
    p = P()
    lmfg = p.inp("lmfg", [D, 128], BF16)
    wlmqkv = p.inp("wlmqkv", [D, 3 * D])
    blmqkvfm = p.inp("blmqkvfm", [128, 48])
    blvrow = p.inp("blvrow", [1, D])
    wlmout = p.inp("wlmout", [D, D])
    blmoutfm = p.inp("blmoutfm", [128, 16])
    wl2tq = p.inp("wl2tq", [D, D])
    bl2tqfm = p.inp("bl2tqfm", [128, 16])
    wl2tout = p.inp("wl2tout", [D, D])
    bl2toutfm = p.inp("bl2toutfm", [128, 16])
    k_fm = p.inp("k_fm", [D, TS], BF16)
    v_tm = p.inp("v_tm", [TS, D], BF16)
    masklm = p.inp("masklm", [64, 128])
    identb = p.inp("identb", [128, 128], BF16)
    lmf3 = p.out("lmf3", [D, 64], BF16)
    lqkv_scr = None
    with p:
        lqkv_scr = p.nc.dram_tensor("lqkv_scr", [3 * D, 128], BF16, kind="Internal").ap()
        lo_scr = p.nc.dram_tensor("lo_scr", [D, 64], BF16, kind="Internal").ap()
        lmf2_scr = p.nc.dram_tensor("lmf2_scr", [D, 64], BF16, kind="Internal").ap()
        l2q_scr = p.nc.dram_tensor("l2q_scr", [D, 64], BF16, kind="Internal").ap()
        ro_scr = p.nc.dram_tensor("ro_scr", [D, 64], BF16, kind="Internal").ap()
        LG = p.sb("LG", [128, 16 * 128], BF16)
        wf = p.sb("wf", [128, 16 * 512])
        wb = p.sb("wb", [128, 16 * 512], BF16)
        LV = p.sb("LV", [128, 2048], BF16)
        bq = p.sb("bq", [128, 48])
        bo = p.sb("bo", [128, 48])
        idb = p.sb("idb", [128, 128], BF16)
        mlm = p.sb("mlm", [64, 128])
        brow = p.sb("brow", [1, D])
        one = p.sb("one", [1, 128])
        ot = p.sb("ot", [128, 512])
        otb = p.sb("otb", [128, 128], BF16)
        st = p.sb("st", [128, 8])
        psA = p.psum("psA", [128, 1024])
        psB = p.psum("psB", [128, 1024])
        w0 = p.load(LG[:].rearrange("p (a c) -> p a c", a=16),
                    lmfg.rearrange("(a p) c -> p a c", p=128))
        w0 += p.load(bq[:], blmqkvfm[:])
        w0 += p.load(idb[:], identb[:])
        w0 += p.load(mlm[:], masklm[:])
        w0 += p.load(brow[:], blvrow[:])
        p.step("vector", lambda e: e.memset(one[:], 1.0), extra=w0)

        def dense(win, bias_col_sb, nout, rhs_sb, nlat, out_scr, add_scr=None,
                  scale=None, bias_pre=None):
            for oc in range(nout):
                w = p.load(wf[:, 0:2048].rearrange("p (a c) -> p a c", a=16),
                           win.rearrange("(a p) c -> p a c", p=128)
                           [:, :, oc * 128:(oc + 1) * 128])
                p.step("vector", lambda e: (e.tensor_copy(wb[:, 0:2048],
                                                          wf[:, 0:2048]),
                                            e.drain()), extra=w)

                def mms(e, nlat=nlat):
                    for k in range(16):
                        e.matmul(psA[:, 0:nlat], wb[:, k * 128:(k + 1) * 128],
                                 rhs_sb[:, k * nlat:(k + 1) * nlat],
                                 start=(k == 0), stop=(k == 15))
                p.step("tensor", mms)

                def evac(e, oc=oc, nlat=nlat, bias_col_sb=bias_col_sb):
                    e.tensor_scalar(ot[:, 0:nlat], psA[:, 0:nlat],
                                    bias_col_sb[:, oc:oc + 1], None, ALU.add)
                    e.drain()
                    e.tensor_copy(otb[:, 0:nlat], ot[:, 0:nlat])
                    e.drain()
                p.step("vector", evac)
                if add_scr is not None:
                    w2 = p.load(wb[:, 0:nlat], add_scr[oc * 128:(oc + 1) * 128, :])
                    p.step("vector", lambda e, nlat=nlat: (
                        e.tensor_tensor(otb[:, 0:nlat], otb[:, 0:nlat],
                                        wb[:, 0:nlat], ALU.add), e.drain()), extra=w2)
                p.store(out_scr[oc * 128:(oc + 1) * 128, :], otb[:, 0:nlat])

        dense(wlmqkv, bq, 48, LG, 128, lqkv_scr)
        for vc in range(4):
            w = p.load(wf[:, 0:16 * 512].rearrange("p (a c) -> p a c", a=16),
                       wlmqkv.rearrange("(a p) c -> p a c", p=128)
                       [:, :, 4096 + vc * 512:4096 + (vc + 1) * 512])
            p.step("vector", lambda e: (e.tensor_copy(wb[:, 0:16 * 512],
                                                      wf[:, 0:16 * 512]), e.drain()),
                   extra=w)

            def vmm(e, vc=vc):
                for k in range(16):
                    e.matmul(psA[:, 0:512], LG[:, k * 128:(k + 1) * 128],
                             wb[:, k * 512:(k + 1) * 512],
                             start=(k == 0), stop=(k == 15))
            p.step("tensor", vmm)
            p.step("tensor", lambda e, vc=vc: e.matmul(
                psB[:, 0:512], one[:], brow[:, vc * 512:(vc + 1) * 512],
                start=True, stop=True))
            p.step("vector", lambda e, vc=vc: (
                e.tensor_copy(ot[:, 0:512], psB[:, 0:512]),
                e.drain(),
                e.tensor_tensor(LV[:, vc * 512:(vc + 1) * 512], psA[:, 0:512],
                                ot[:, 0:512], ALU.add), e.drain()))

        qh2 = p.sb("qh2", [64, 64], BF16)
        kh2 = p.sb("kh2", [64, 128], BF16)
        sc2 = p.sb("sc2", [64, 128])
        st2 = p.sb("st2", [64, 4])
        pb2 = p.sb("pb2", [64, 128], BF16)
        ptb2 = p.sb("ptb2", [128, 64], BF16)
        ob2 = p.sb("ob2", [64, 64], BF16)
        for h in range(32):
            w = p.load(qh2[:], lqkv_scr[h * 64:(h + 1) * 64, 0:64])
            w += p.load(kh2[:], lqkv_scr[2048 + h * 64:2048 + (h + 1) * 64, :])
            p.step("tensor", lambda e: e.matmul(psA[0:64, 0:128], qh2[:], kh2[:],
                                                start=True, stop=True), extra=w)

            def sm(e):
                e.tensor_scalar(sc2[:], psA[0:64, 0:128], 0.125, None, ALU.mult)
                e.drain()
                e.tensor_tensor(sc2[:], sc2[:], mlm[:], ALU.add)
                e.drain()
                e.tensor_reduce(st2[:, 0:1], sc2[:], AX.X, ALU.max)
                e.drain()
                e.tensor_scalar(sc2[:], sc2[:], st2[:, 0:1], None, ALU.subtract)
                e.drain()
            p.step("vector", sm)
            p.step("scalar", lambda e: e.activation(sc2[:], sc2[:], AF.Exp))

            def nrm(e):
                e.tensor_reduce(st2[:, 1:2], sc2[:], AX.X, ALU.add)
                e.drain()
                e.reciprocal(st2[:, 1:2], st2[:, 1:2])
                e.drain()
                e.tensor_scalar(pb2[:], sc2[:], st2[:, 1:2], None, ALU.mult)
                e.drain()
            p.step("vector", nrm)
            p.step("tensor", lambda e: e.matmul(psB[0:128, 0:64], pb2[:],
                                                idb[0:64, 0:64], start=True, stop=True))
            p.step("vector", lambda e: (e.tensor_copy(ptb2[:], psB[0:128, 0:64]),
                                        e.drain()))
            p.step("tensor", lambda e, h=h: e.matmul(
                psA[0:64, 0:64], LV[:, h * 64:(h + 1) * 64], ptb2[:],
                start=True, stop=True))
            p.step("vector", lambda e: (e.tensor_copy(ob2[:], psA[0:64, 0:64]),
                                        e.drain()))
            p.store(lo_scr[h * 64:(h + 1) * 64, :], ob2[:])

        LO = p.sb("LO", [128, 16 * 64], BF16)
        w = p.load(LO[:].rearrange("p (a c) -> p a c", a=16),
                   lo_scr.rearrange("(a p) c -> p a c", p=128))
        p.step("vector", lambda e: None, extra=w)
        w = p.load(bo[:, 0:16], blmoutfm[:])
        p.step("vector", lambda e: None, extra=w)
        LMOWN = p.sb("LMOWN", [128, 16 * 64], BF16)
        w = p.load(LMOWN[:].rearrange("p (a c) -> p a c", a=16),
                   lmfg.rearrange("(a p) c -> p a c", p=128)[:, :, 0:64])
        p.step("vector", lambda e: None, extra=w)

        def dense2(win, bfm_sb, rhs_sb, out_scr, addsb=None, scale=None,
                   bprescale=None, boff=0):
            for oc in range(16):
                w2 = p.load(wf[:, 0:2048].rearrange("p (a c) -> p a c", a=16),
                            win.rearrange("(a p) c -> p a c", p=128)
                            [:, :, oc * 128:(oc + 1) * 128])
                p.step("vector", lambda e: (e.tensor_copy(wb[:, 0:2048],
                                                          wf[:, 0:2048]),
                                            e.drain()), extra=w2)

                def mms(e):
                    for k in range(16):
                        e.matmul(psA[:, 0:64], wb[:, k * 128:(k + 1) * 128],
                                 rhs_sb[:, k * 64:(k + 1) * 64],
                                 start=(k == 0), stop=(k == 15))
                p.step("tensor", mms)

                def evac(e, oc=oc):
                    e.tensor_scalar(ot[:, 0:64], psA[:, 0:64],
                                    bfm_sb[:, boff + oc:boff + oc + 1], None,
                                    ALU.add)
                    e.drain()
                p.step("vector", evac)

                def cst(e, oc=oc, addsb=addsb):
                    if addsb is not None:
                        e.tensor_tensor(otb[:, 0:64], ot[:, 0:64],
                                        addsb[:, oc * 64:(oc + 1) * 64], ALU.add)
                    else:
                        e.tensor_copy(otb[:, 0:64], ot[:, 0:64])
                    e.drain()
                p.step("vector", cst)
                p.store(out_scr[oc * 128:(oc + 1) * 128, :], otb[:, 0:64])

        dense2(wlmout, bo, LO, lmf2_scr, addsb=LMOWN)
        LM2 = p.sb("LM2", [128, 16 * 64], BF16)
        w = p.load(LM2[:].rearrange("p (a c) -> p a c", a=16),
                   lmf2_scr.rearrange("(a p) c -> p a c", p=128))
        p.step("vector", lambda e: None, extra=w)
        w = p.load(bo[:, 16:32], bl2tqfm[:])
        p.step("vector", lambda e: None, extra=w)
        bo2 = bo[:, 16:32]

        def dense3(win, bview, rhs_sb, out_scr, scale, bprescale):
            for oc in range(16):
                w2 = p.load(wf[:, 0:2048].rearrange("p (a c) -> p a c", a=16),
                            win.rearrange("(a p) c -> p a c", p=128)
                            [:, :, oc * 128:(oc + 1) * 128])
                p.step("vector", lambda e: (e.tensor_copy(wb[:, 0:2048],
                                                          wf[:, 0:2048]),
                                            e.drain()), extra=w2)

                def mms(e):
                    for k in range(16):
                        e.matmul(psA[:, 0:64], wb[:, k * 128:(k + 1) * 128],
                                 rhs_sb[:, k * 64:(k + 1) * 64],
                                 start=(k == 0), stop=(k == 15))
                p.step("tensor", mms)

                def evac(e, oc=oc):
                    e.tensor_scalar(ot[:, 0:64], psA[:, 0:64],
                                    bview[:, oc:oc + 1], scale, ALU.add, ALU.mult)
                    e.drain()
                p.step("vector", evac)
                p.step("vector", lambda e: (e.tensor_copy(otb[:, 0:64], ot[:, 0:64]),
                                            e.drain()))
                p.store(out_scr[oc * 128:(oc + 1) * 128, :], otb[:, 0:64])

        dense3(wl2tq, bo2, LM2, l2q_scr, 0.125, 0.125)

        vh3 = v_tm.rearrange("(blk s) d -> s blk d", s=64)
        qh3 = p.sb("qh3", [64, 64], BF16)
        kh3 = p.sb("kh3", [64, TS], BF16)
        vh4 = p.sb("vh4", [64, 16 * 64], BF16)
        sc3 = p.sb("sc3", [64, TS])
        st3 = p.sb("st3", [64, 32])
        pb3 = p.sb("pb3", [64, TS], BF16)
        pat3 = p.sb("pat3", [64, 64], BF16)
        ro3 = p.sb("ro3", [64, 64], BF16)
        for h in range(32):
            w = p.load(qh3[:], l2q_scr[h * 64:(h + 1) * 64, :])
            w += p.load(kh3[:], k_fm[h * 64:(h + 1) * 64, :])
            w += p.load(vh4[:].rearrange("p (a c) -> p a c", a=16),
                        vh3[:, :, h * 64:(h + 1) * 64])

            def smm(e):
                for blk in range(16):
                    e.matmul(psA[0:4, blk * 64:(blk + 1) * 64],
                             qh3[:, blk * 4:(blk + 1) * 4],
                             kh3[:, blk * 64:(blk + 1) * 64], start=True, stop=True)
            p.step("tensor", smm, extra=w)

            def sm(e):
                e.tensor_copy(sc3[0:4, :], psA[0:4, :])
                e.drain()
                s3 = sc3[0:4, :].rearrange("p (b c) -> p b c", b=16)
                mx = st3[0:4, 0:16].rearrange("p (b c) -> p b c", c=1)
                e.tensor_reduce(mx, s3, AX.X, ALU.max)
                e.drain()
                _, mxb = bass.broadcast_tensor_aps(s3, mx)
                e.tensor_tensor(s3, s3, mxb, ALU.subtract)
                e.drain()
            p.step("vector", sm)
            p.step("scalar", lambda e: e.activation(sc3[0:4, :], sc3[0:4, :], AF.Exp))

            def nrm(e):
                s3 = sc3[0:4, :].rearrange("p (b c) -> p b c", b=16)
                sm_ = st3[0:4, 16:32].rearrange("p (b c) -> p b c", c=1)
                e.tensor_reduce(sm_, s3, AX.X, ALU.add)
                e.drain()
                e.reciprocal(st3[0:4, 16:32], st3[0:4, 16:32])
                e.drain()
                sm2 = st3[0:4, 16:32].rearrange("p (b c) -> p b c", c=1)
                _, smb = bass.broadcast_tensor_aps(s3, sm2)
                p3 = pb3[0:4, :].rearrange("p (b c) -> p b c", b=16)
                e.tensor_tensor(p3, s3, smb, ALU.mult)
                e.drain()
            p.step("vector", nrm)

            def patr(e):
                for blk in range(16):
                    e.matmul(psB[0:64, blk * 4:(blk + 1) * 4],
                             pb3[0:4, blk * 64:(blk + 1) * 64],
                             idb[0:4, 0:4], start=True, stop=True)
            p.step("tensor", patr)
            p.step("vector", lambda e: (e.tensor_copy(pat3[:, 0:64], psB[0:64, 0:64]),
                                        e.drain()))

            def rmm(e):
                for blk in range(16):
                    e.matmul(psA[0:64, blk * 4:(blk + 1) * 4],
                             vh4[:, blk * 64:(blk + 1) * 64],
                             pat3[:, blk * 4:(blk + 1) * 4], start=True, stop=True)
            p.step("tensor", rmm)
            p.step("vector", lambda e: (e.tensor_copy(ro3[:, 0:64], psA[0:64, 0:64]),
                                        e.drain()))
            p.store(ro_scr[h * 64:(h + 1) * 64, :], ro3[:, 0:64])

        RO = p.sb("RO", [128, 16 * 64], BF16)
        w = p.load(RO[:].rearrange("p (a c) -> p a c", a=16),
                   ro_scr.rearrange("(a p) c -> p a c", p=128))
        p.step("vector", lambda e: None, extra=w)
        LM2b = p.sb("LM2b", [128, 16 * 64], BF16)
        w = p.load(LM2b[:].rearrange("p (a c) -> p a c", a=16),
                   lmf2_scr.rearrange("(a p) c -> p a c", p=128))
        p.step("vector", lambda e: None, extra=w)
        w = p.load(bo[:, 32:48], bl2toutfm[:])
        p.step("vector", lambda e: None, extra=w)
        dense2(wl2tout, bo, RO, lmf3, addsb=LM2b, boff=32)
        p.finish()
    return p

def build_global():
    p = P()
    lmf3g = p.inp("lmf3g", [D, 128], BF16)
    wlmkv = p.inp("wlmkv", [D, 2 * D])
    bgkfm = p.inp("bgkfm", [128, 16])
    bgvrow = p.inp("bgvrow", [1, D])
    q_fm = p.inp("q_fm", [D, TS], BF16)
    olocal = p.inp("olocal", [D, TS], BF16)
    gfm = p.inp("gfm", [128, 8 * 32])
    maskg = p.inp("maskg", [128, 8 * 128])
    zrec = p.inp("zrec", [128, 8])
    identb = p.inp("identb", [128, 128], BF16)
    oat = p.out("oat", [D, TS], BF16)
    with p:
        gk_scr = p.nc.dram_tensor("gk_scr", [D, 128], BF16, kind="Internal").ap()
        LG = p.sb("LG", [128, 16 * 128], BF16)
        wf = p.sb("wf", [128, 16 * 512])
        wb = p.sb("wb", [128, 16 * 512], BF16)
        GV = p.sb("GV", [128, 2048], BF16)
        bgk = p.sb("bgk", [128, 16])
        brow = p.sb("brow", [1, D])
        one = p.sb("one", [1, 128])
        gfms = p.sb("gfms", [128, 8 * 32])
        mg = p.sb("mg", [128, 8 * 128])
        zr = p.sb("zr", [128, 8])
        idb = p.sb("idb", [128, 128], BF16)
        ot = p.sb("ot", [128, 128])
        ot2g = p.sb("ot2g", [128, 512])
        otb = p.sb("otb", [128, 128], BF16)
        qh = p.sb("qh", [64, 128], BF16)
        gkh = p.sb("gkh", [64, 128], BF16)
        olh = p.sb("olh", [64, 128], BF16)
        sc_ = p.sb("sc_", [128, 128])
        st = p.sb("st", [128, 4])
        pbx = p.sb("pbx", [128, 128], BF16)
        ptbx = p.sb("ptbx", [128, 128], BF16)
        psA = p.psum("psA", [128, 1024])
        psB = p.psum("psB", [128, 1024])
        w0 = p.load(LG[:].rearrange("p (a c) -> p a c", a=16),
                    lmf3g.rearrange("(a p) c -> p a c", p=128))
        w0 += p.load(bgk[:], bgkfm[:])
        w0 += p.load(brow[:], bgvrow[:])
        w0 += p.load(gfms[:], gfm[:])
        w0 += p.load(mg[:], maskg[:])
        w0 += p.load(zr[:], zrec[:])
        w0 += p.load(idb[:], identb[:])
        p.step("vector", lambda e: e.memset(one[:], 1.0), extra=w0)
        for oc in range(16):
            w = p.load(wf[:, 0:2048].rearrange("p (a c) -> p a c", a=16),
                       wlmkv.rearrange("(a p) c -> p a c", p=128)
                       [:, :, oc * 128:(oc + 1) * 128])
            p.step("vector", lambda e: (e.tensor_copy(wb[:, 0:2048],
                                                      wf[:, 0:2048]),
                                        e.drain()), extra=w)

            def mms(e):
                for k in range(16):
                    e.matmul(psA[:, 0:128], wb[:, k * 128:(k + 1) * 128],
                             LG[:, k * 128:(k + 1) * 128],
                             start=(k == 0), stop=(k == 15))
            p.step("tensor", mms)
            p.step("vector", lambda e, oc=oc: (
                e.tensor_scalar(ot[:], psA[:, 0:128], bgk[:, oc:oc + 1], None,
                                ALU.add), e.drain()))
            p.step("vector", lambda e: (e.tensor_copy(otb[:], ot[:]), e.drain()))
            p.store(gk_scr[oc * 128:(oc + 1) * 128, :], otb[:])
        for vc in range(4):
            w = p.load(wf[:, 0:16 * 512].rearrange("p (a c) -> p a c", a=16),
                       wlmkv.rearrange("(a p) c -> p a c", p=128)
                       [:, :, 2048 + vc * 512:2048 + (vc + 1) * 512])
            p.step("vector", lambda e: (e.tensor_copy(wb[:, 0:16 * 512],
                                                      wf[:, 0:16 * 512]), e.drain()),
                   extra=w)

            def vmm(e):
                for k in range(16):
                    e.matmul(psA[:, 0:512], LG[:, k * 128:(k + 1) * 128],
                             wb[:, k * 512:(k + 1) * 512],
                             start=(k == 0), stop=(k == 15))
            p.step("tensor", vmm)
            p.step("tensor", lambda e, vc=vc: e.matmul(
                psB[:, 0:512], one[:], brow[:, vc * 512:(vc + 1) * 512],
                start=True, stop=True))
            p.step("vector", lambda e, vc=vc: (
                e.tensor_copy(ot2g[:, 0:512], psB[:, 0:512]),
                e.drain(),
                e.tensor_tensor(GV[:, vc * 512:(vc + 1) * 512], psA[:, 0:512],
                                ot2g[:, 0:512], ALU.add), e.drain()))
        for h in range(32):
            wk = p.load(gkh[:], gk_scr[h * 64:(h + 1) * 64, :])
            for tc in range(8):
                w = p.load(qh[:], q_fm[h * 64:(h + 1) * 64, tc * 128:(tc + 1) * 128])
                w += p.load(olh[:], olocal[h * 64:(h + 1) * 64,
                                           tc * 128:(tc + 1) * 128])
                p.step("tensor", lambda e: e.matmul(psA[:, 0:128], qh[:], gkh[:],
                                                    start=True, stop=True),
                       extra=w + (wk if tc == 0 else []))

                def sm(e, tc=tc):
                    e.tensor_tensor(sc_[:], psA[:, 0:128],
                                    mg[:, tc * 128:(tc + 1) * 128], ALU.add)
                    e.drain()
                    e.tensor_reduce(st[:, 0:1], sc_[:], AX.X, ALU.max)
                    e.drain()
                    e.tensor_scalar(sc_[:], sc_[:], st[:, 0:1], None, ALU.subtract)
                    e.drain()
                p.step("vector", sm)
                p.step("scalar", lambda e: e.activation(sc_[:], sc_[:], AF.Exp))

                def nrm(e, h=h, tc=tc):
                    e.tensor_reduce(st[:, 1:2], sc_[:], AX.X, ALU.add)
                    e.drain()
                    e.reciprocal(st[:, 1:2], st[:, 1:2])
                    e.drain()
                    e.tensor_tensor(st[:, 1:2], st[:, 1:2],
                                    gfms[:, tc * 32 + h:tc * 32 + h + 1], ALU.mult)
                    e.drain()
                    e.tensor_tensor(st[:, 1:2], st[:, 1:2], zr[:, tc:tc + 1],
                                    ALU.mult)
                    e.drain()
                    e.tensor_scalar(pbx[:], sc_[:], st[:, 1:2], None, ALU.mult)
                    e.drain()
                p.step("vector", nrm)
                p.step("tensor", lambda e: e.matmul(psB[:, 0:128], pbx[:], idb[:],
                                                    start=True, stop=True))
                p.step("vector", lambda e: (e.tensor_copy(ptbx[:], psB[:, 0:128]),
                                            e.drain()))
                p.step("tensor", lambda e, h=h: e.matmul(
                    psA[0:64, 0:128], GV[:, h * 64:(h + 1) * 64], ptbx[:],
                    start=True, stop=True))
                p.step("vector", lambda e: (
                    e.tensor_tensor(otb[0:64, :], psA[0:64, 0:128], olh[:], ALU.add),
                    e.drain()))
                p.store(oat[h * 64:(h + 1) * 64, tc * 128:(tc + 1) * 128],
                        otb[0:64, :])
        p.finish()
    return p


def build_wout():
    p = P()
    oat = p.inp("oat", [D, TS], BF16)
    wout = p.inp("wout", [D, D])
    boutfm = p.inp("boutfm", [128, 16])
    attnp = p.out("attnp", [D, TS])
    with p:
        OA = p.sb("OA", [128, 16 * TS], BF16)
        wf = p.sb("wf", [128, 16 * 128])
        wb = p.sb("wb", [128, 16 * 128], BF16)
        bo = p.sb("bo", [128, 16])
        ot = p.sb("ot", [128, TS])
        psA = p.psum("psA", [128, 1024])
        w0 = p.load(OA[:].rearrange("p (a c) -> p a c", a=16),
                    oat.rearrange("(a p) c -> p a c", p=128))
        w0 += p.load(bo[:], boutfm[:])
        p.step("vector", lambda e: None, extra=w0)
        for oc in range(16):
            w = p.load(wf[:].rearrange("p (a c) -> p a c", a=16),
                       wout.rearrange("(a p) c -> p a c", p=128)
                       [:, :, oc * 128:(oc + 1) * 128])
            p.step("vector", lambda e: (e.tensor_copy(wb[:], wf[:]), e.drain()),
                   extra=w)

            def mms(e):
                for k in range(16):
                    for th in range(2):
                        e.matmul(psA[:, th * 512:(th + 1) * 512],
                                 wb[:, k * 128:(k + 1) * 128],
                                 OA[:, k * TS + th * 512:k * TS + (th + 1) * 512],
                                 start=(k == 0), stop=(k == 15))
            p.step("tensor", mms)
            p.step("vector", lambda e, oc=oc: (
                e.tensor_scalar(ot[:], psA[:], bo[:, oc:oc + 1], None, ALU.add),
                e.drain()))
            p.store(attnp[oc * 128:(oc + 1) * 128, :], ot[:])
        p.finish()
    return p

def build_glu():
    p = P()
    qxn1T = p.inp("qxn1T", [D, TS], BF16)
    qwgT = p.inp("qwgT", [D, IF], BF16)
    qwuT = p.inp("qwuT", [D, IF], BF16)
    gsubrow = p.inp("gsubrow", [1, IF])
    fcl8 = p.inp("fcl8", [128, 8])
    s3x8 = p.inp("s3x8", [8, 3])
    qacts = p.out("qacts", [TS, IF], BF16)
    with p:
        actg_scr = p.nc.dram_tensor("actg_scr", [TS, IF], F32, kind="Internal").ap()
        QX = p.sb("QX", [128, 16 * TS], BF16)
        wgb = p.sb("wgb", [128, 16 * 512], BF16)
        wub = p.sb("wub", [128, 16 * 512], BF16)
        grep = p.sb("grep", [128, 512])
        gsr = p.sb("gsr", [1, IF])
        one = p.sb("one", [1, 128])
        fcl = p.sb("fcl", [128, 8])
        s8 = p.sb("s8", [8, 3])
        mrow = p.sb("mrow", [1, 8])
        m3 = p.sb("m3", [1, 3])
        cc = p.sb("cc", [128, 16])
        gr = p.sb("gr", [128, 512])
        up = p.sb("up", [128, 512])
        act = p.sb("act", [128, 512])
        sq = p.sb("sq", [128, 512])
        ta = p.sb("ta", [128, 512]); tb = p.sb("tb", [128, 512])
        ti = p.sb("ti", [128, 512], I32)
        qo = p.sb("qo", [128, 512], BF16)
        vacc = p.sb("vacc", [128, 8])
        macc = p.sb("macc", [128, 8])
        psA = p.psum("psA", [128, 1024])
        psB = p.psum("psB", [128, 1024])
        scr83 = p.nc.dram_tensor("scr83", [8, 3], F32, kind="Internal").ap()
        w0 = p.load(gsr[:], gsubrow[:])
        w0 += p.load(fcl[:], fcl8[:])
        w0 += p.load(s8[:], s3x8[:])
        p.step("vector", lambda e: (e.memset(one[:], 1.0), e.drain(),
                                    e.memset(vacc[:], 0.0), e.drain(),
                                    e.memset(macc[:], 0.0), e.drain()), extra=w0)
        p.store(scr83[:], s8[:])
        for j in range(3):
            wj = p.load(mrow[0:1, 0:8], scr83[:, j:j + 1].rearrange("p a -> a p"), slow=True)
            p.step("vector", lambda e, j=j: (
                e.tensor_reduce(m3[:, j:j + 1], mrow[:], AX.X, ALU.add),
                e.drain(),
                e.tensor_scalar(m3[:, j:j + 1], m3[:, j:j + 1], 1.0 / (IF * D),
                                1e-5, ALU.mult, ALU.max),
                e.drain()), extra=wj)

        p.step("tensor", lambda e: e.matmul(psA[:, 0:3], one[:, 0:128],
                                            m3[:, 0:3], start=True, stop=True))
        p.step("vector", lambda e: (e.tensor_copy(cc[:, 0:3], psA[:, 0:3]),
                                    e.drain()))
        def mkc(e):
            e.tensor_scalar(cc[:, 4:12], fcl[:], 1.0 / 127.0, None, ALU.mult)
            e.drain()
        p.step("vector", mkc)
        for kc in range(16):
            w = p.load(QX[:, kc * TS:(kc + 1) * TS], qxn1T[kc * 128:(kc + 1) * 128, :])
            if kc == 15:
                p.step("vector", lambda e: None, extra=w)
        for icol in range(16):
            w = p.load(wgb[:].rearrange("p (a c) -> p a c", a=16),
                       qwgT.rearrange("(a p) c -> p a c", p=128)
                       [:, :, icol * 512:(icol + 1) * 512])
            w += p.load(wub[:].rearrange("p (a c) -> p a c", a=16),
                        qwuT.rearrange("(a p) c -> p a c", p=128)
                        [:, :, icol * 512:(icol + 1) * 512])
            p.step("tensor", lambda e, icol=icol: e.matmul(
                psB[:, 0:512], one[:], gsr[:, icol * 512:(icol + 1) * 512],
                start=True, stop=True), extra=w)
            p.step("vector", lambda e: (e.tensor_copy(grep[:], psB[:, 0:512]),
                                        e.drain()))
            for tc in range(8):
                def mms(e, tc=tc):
                    for k in range(16):
                        e.matmul(psA[:, 0:512],
                                 QX[:, k * TS + tc * 128:k * TS + (tc + 1) * 128],
                                 wgb[:, k * 512:(k + 1) * 512],
                                 start=(k == 0), stop=(k == 15))
                    for k in range(16):
                        e.matmul(psA[:, 512:1024],
                                 QX[:, k * TS + tc * 128:k * TS + (tc + 1) * 128],
                                 wub[:, k * 512:(k + 1) * 512],
                                 start=(k == 0), stop=(k == 15))
                p.step("tensor", mms)

                def cgc(e, tc=tc):
                    e.tensor_tensor(cc[:, 12:13], cc[:, 4 + tc:5 + tc], cc[:, 0:1],
                                    ALU.mult)
                    e.drain()
                    e.tensor_tensor(cc[:, 13:14], cc[:, 4 + tc:5 + tc], cc[:, 1:2],
                                    ALU.mult)
                    e.drain()
                p.step("vector", cgc)
                p.step("vector", lambda e: (
                    e.tensor_scalar(gr[:], psA[:, 0:512], cc[:, 12:13], 0.0,
                                    ALU.mult, ALU.max),
                    e.drain(),
                    e.tensor_scalar(up[:], psA[:, 512:1024], cc[:, 13:14], None,
                                    ALU.mult),
                    e.drain()))
                p.step("scalar", lambda e: (
                    e.activation(gr[:], gr[:], AF.Square),
                    e.drain()))

                def actops(e, tc=tc):
                    e.tensor_tensor(act[:], up[:], gr[:], ALU.mult)
                    e.drain()
                    e.tensor_tensor(ta[:], act[:], act[:], ALU.mult)
                    e.drain()
                    e.tensor_reduce(tb[:, 0:1], ta[:], AX.X, ALU.add)
                    e.drain()
                    e.tensor_tensor(vacc[:, tc:tc + 1], vacc[:, tc:tc + 1],
                                    tb[:, 0:1], ALU.add)
                    e.drain()
                    e.tensor_tensor(act[:], act[:], grep[:], ALU.mult)
                    e.drain()
                    e.tensor_reduce(tb[:, 0:1], act[:], AX.X, ALU.max,
                                    apply_absolute_value=True)
                    e.drain()
                    e.tensor_tensor(macc[:, tc:tc + 1], macc[:, tc:tc + 1],
                                    tb[:, 0:1], ALU.max)
                    e.drain()
                p.step("vector", actops)
                p.store(actg_scr[tc * 128:(tc + 1) * 128,
                                 icol * 512:(icol + 1) * 512], act[:])
        def scal(e):
            e.tensor_scalar(vacc[:], vacc[:], 1.0 / IF, 1e-6, ALU.mult, ALU.add)
            e.drain()
        p.step("vector", scal)
        p.step("scalar", lambda e: e.activation(cc[:, 4:12], vacc[:], AF.Sqrt))

        def scal2(e):
            e.reciprocal(ta[:, 0:8], cc[:, 4:12])
            e.drain()
            e.tensor_tensor(ta[:, 0:8], vacc[:], ta[:, 0:8], ALU.mult)
            e.drain()
            e.tensor_tensor(ta[:, 0:8], ta[:, 0:8], cc[:, 4:12], ALU.add)
            e.drain()
            e.tensor_scalar(ta[:, 0:8], ta[:, 0:8], 0.5, None, ALU.mult)
            e.drain()
            e.reciprocal(ta[:, 0:8], ta[:, 0:8])
            e.drain()
            e.tensor_tensor(tb[:, 0:8], ta[:, 0:8], macc[:], ALU.mult)
            e.drain()
            e.tensor_scalar(tb[:, 0:8], tb[:, 0:8], 1e-5, None, ALU.max)
            e.drain()
            recip_newton(e, cc[:, 4:12], tb[:, 0:8], ta[:, 8:16] if False else
                         sq[:, 0:8], sq[:, 8:16])
            e.tensor_scalar(cc[:, 4:12], cc[:, 4:12], 127.0, None, ALU.mult)
            e.drain()
            e.tensor_tensor(cc[:, 4:12], cc[:, 4:12], ta[:, 0:8], ALU.mult)
            e.drain()
            e.tensor_scalar(tb[:, 0:8], tb[:, 0:8], 1.0 / 127.0, None, ALU.mult)
            e.drain()
            e.tensor_scalar(cc[:, 2:3], cc[:, 2:3], 1.0, None, ALU.mult)
            e.drain()
            for tc8 in range(8):
                e.tensor_tensor(vacc[:, tc8:tc8 + 1], tb[:, tc8:tc8 + 1],
                                cc[:, 2:3], ALU.mult)
                e.drain()
        p.step("vector", scal2)
        for tc in range(8):
            for icol in range(16):
                w = p.load(act[:], actg_scr[tc * 128:(tc + 1) * 128,
                                            icol * 512:(icol + 1) * 512])

                def qz(e, tc=tc):
                    quant_steps(p, e, act[:], cc[:, 4 + tc:5 + tc], qo[:],
                                ta[:], tb[:], ti[:])
                    e.tensor_scalar(qo[:], qo[:], vacc[:, tc:tc + 1], None,
                                    ALU.mult)
                    e.drain()
                p.step("vector", qz, extra=w)
                p.store(qacts[tc * 128:(tc + 1) * 128,
                              icol * 512:(icol + 1) * 512], qo[:])
        p.finish()
    return p


def build_down_final():
    p = P()
    qactsT = p.inp("qactsT", [IF, TS], BF16)
    qwdT = p.inp("qwdT", [IF, D], BF16)
    x1T = p.inp("x1T", [D, TS])
    attnp = p.inp("attnp", [D, TS])
    x2 = p.out("x2", [D, TS])
    with p:
        QA = p.sb("QA", [128, 16 * TS], BF16)
        wb = p.sb("wb", [128, 16 * 128], BF16)
        OACC = p.sb("OACC", [128, 16 * TS])
        xt = p.sb("xt", [128, TS])
        psA = p.psum("psA", [128, 1024])
        p.step("vector", lambda e: e.memset(OACC[:], 0.0))
        for og in range(4):
            w0 = p.load(QA[:].rearrange("p (a c) -> p a c", a=16),
                        qactsT.rearrange("(a p) c -> p a c", p=128)
                        [:, og * 16:(og + 1) * 16, :])
            p.step("vector", lambda e: None, extra=w0)
            for oc in range(16):
                w = p.load(wb[:].rearrange("p (a c) -> p a c", a=16),
                           qwdT.rearrange("(a p) c -> p a c", p=128)
                           [:, og * 16:(og + 1) * 16, oc * 128:(oc + 1) * 128])

                def mms(e):
                    for k in range(16):
                        for th in range(2):
                            e.matmul(psA[:, th * 512:(th + 1) * 512],
                                     wb[:, k * 128:(k + 1) * 128],
                                     QA[:, k * TS + th * 512:k * TS + (th + 1) * 512],
                                     start=(k == 0), stop=(k == 15))
                p.step("tensor", mms, extra=w)
                p.step("vector", lambda e, oc=oc: (
                    e.tensor_tensor(OACC[:, oc * TS:(oc + 1) * TS],
                                    OACC[:, oc * TS:(oc + 1) * TS], psA[:],
                                    ALU.add), e.drain()))
        for oc in range(16):
            w = p.load(xt[:], x1T[oc * 128:(oc + 1) * 128, :])
            p.step("vector", lambda e, oc=oc: (
                e.tensor_tensor(OACC[:, oc * TS:(oc + 1) * TS],
                                OACC[:, oc * TS:(oc + 1) * TS], xt[:], ALU.add),
                e.drain()), extra=w)
            w = p.load(xt[:], attnp[oc * 128:(oc + 1) * 128, :])
            p.step("vector", lambda e, oc=oc: (
                e.tensor_tensor(OACC[:, oc * TS:(oc + 1) * TS],
                                OACC[:, oc * TS:(oc + 1) * TS], xt[:], ALU.add),
                e.drain()), extra=w)
            p.store(x2[oc * 128:(oc + 1) * 128, :], OACC[:, oc * TS:(oc + 1) * TS])
        p.finish()
    return p

_CACHE = {}


def _bld(name, fn):
    if name not in _CACHE:
        _CACHE[name] = fn()
    return _CACHE[name]


def _T(a):
    return np.ascontiguousarray(a.T)


def _C(a):
    return np.ascontiguousarray(a)


def kernel(L0_attn_x, L0_attn_x_proj_w, L0_attn_dt_proj_w, L0_attn_dt_proj_b,
           L0_attn_A_log, L0_attn_D, L0_ffn_w1, L0_ffn_b1, L0_ffn_w2, L0_ffn_b2,
           L1_attn_w_qkv, L1_attn_b_qkv, L1_attn_pool_q, L1_attn_w_lm_qkv,
           L1_attn_b_lm_qkv, L1_attn_w_lm_out, L1_attn_b_lm_out, L1_attn_w_l2t_q,
           L1_attn_b_l2t_q, L1_attn_w_l2t_out, L1_attn_b_l2t_out, L1_attn_w_lm_kv,
           L1_attn_b_lm_kv, L1_attn_w_out, L1_attn_b_out, L1_attn_w_gate,
           L1_attn_b_gate, L1_ffn_w_gate, L1_ffn_w_up, L1_ffn_w_down,
           L1_ffn_subln_weight):
    x = np.asarray(L0_attn_x, dtype=np.float32)
    xsh = [_C(x[c // 2, (c % 2) * TS:(c % 2 + 1) * TS, :]) for c in range(8)]

    pL = _bld("ln", build_ln)
    r = pL.run([{"x": xsh[c]} for c in range(8)])
    xn = [r[c]["xn"] for c in range(8)]
    mxs = _C(np.concatenate([r[c]["mx"] for c in range(8)], axis=1))

    wg = np.asarray(L1_ffn_w_gate, np.float32)
    wu = np.asarray(L1_ffn_w_up, np.float32)
    wd = np.asarray(L1_ffn_w_down, np.float32)
    pS = _bld("wsums", build_wsums)
    r = pS.run([{"wgs": _C(wg[c * 1024:(c + 1) * 1024]),
                 "wus": _C(wu[c * 1024:(c + 1) * 1024]),
                 "wds": _C(wd[c * 256:(c + 1) * 256])} for c in range(8)])
    s3x8 = _C(np.concatenate([r[c]["s3"] for c in range(8)], axis=0))

    w1 = np.asarray(L0_ffn_w1, np.float32)
    w2 = np.asarray(L0_ffn_w2, np.float32)
    pQ12 = _bld("wq12", build_wquant12)
    r = pQ12.run([{"w1s": _C(w1[c * 1024:(c + 1) * 1024]),
                   "w2s": _C(w2[c * 256:(c + 1) * 256])} for c in range(8)])
    qw1 = np.concatenate([r[c]["qw1s"] for c in range(8)], axis=0)
    qw2 = np.concatenate([r[c]["qw2s"] for c in range(8)], axis=0)
    sw1 = np.concatenate([r[c]["s1"] for c in range(8)], axis=0).ravel()
    sw2 = np.concatenate([r[c]["s2"] for c in range(8)], axis=0).ravel()
    qw1T = _T(qw1)
    qw2T = _T(qw2)
    sw1fm = _T(sw1.reshape(64, 128))
    sw2fm = _T(sw2.reshape(16, 128))

    pQG = _bld("wqg", build_wquantg)
    r = pQG.run([{"s3x8": s3x8,
                  "wgs": _C(wg[c * 1024:(c + 1) * 1024]),
                  "wus": _C(wu[c * 1024:(c + 1) * 1024]),
                  "wds": _C(wd[c * 256:(c + 1) * 256])} for c in range(8)])
    qwgT = _T(np.concatenate([r[c]["qg"] for c in range(8)], axis=0))
    qwuT = _T(np.concatenate([r[c]["qu"] for c in range(8)], axis=0))
    qwdT = _T(np.concatenate([r[c]["qd"] for c in range(8)], axis=0))

    b1fm = _T(np.asarray(L0_ffn_b1, np.float32).reshape(64, 128))
    b2fm = _T(np.asarray(L0_ffn_b2, np.float32).reshape(16, 128))
    xnT = [_T(a) for a in xn]
    pF1 = _bld("ffn1", build_ffn1)
    r = pF1.run([{"xnT": xnT[c], "qw1T": qw1T, "sw1fm": sw1fm, "b1fm": b1fm,
                  "mxs": mxs} for c in range(8)])
    h1 = [r[c]["h1"] for c in range(8)]
    hmaxs = _C(np.concatenate([r[c]["hmax"] for c in range(8)], axis=1))
    pF2 = _bld("ffn2", build_ffn2)
    r = pF2.run([{"h1": h1[c], "qw2T": qw2T, "sw2fm": sw2fm, "b2fm": b2fm,
                  "hmaxs": hmaxs} for c in range(8)])
    o2t = [_T(r[c]["o2"]) for c in range(8)]

    xpwT = _T(np.asarray(L0_attn_x_proj_w, np.float32))
    pXD = _bld("xdbl", build_xdbl)
    r = pXD.run([{"xnT": xnT[c], "xpwT": xpwT} for c in range(8)])
    xdg = np.concatenate([r[c]["xd"] for c in range(8)], axis=1)
    xn_all = np.concatenate(xn, axis=0)
    dtw = np.asarray(L0_attn_dt_proj_w, np.float32)
    dtb = np.asarray(L0_attn_dt_proj_b, np.float32)
    alog = np.asarray(L0_attn_A_log, np.float32)
    dv = np.asarray(L0_attn_D, np.float32)
    pM = _bld("mamba", build_mamba)
    ins = []
    for c in range(8):
        sl = slice(c * 256, (c + 1) * 256)
        ins.append({
            "xnS": _T(xn_all[:, sl]),
            "dtwTS": _T(dtw[sl]),
            "dtbS": _T(dtb[sl].reshape(2, 128)),
            "alogS": _C(alog[sl].reshape(2, 128, 16).transpose(1, 0, 2)
                        .reshape(128, 32)),
            "dS": _T(dv[sl].reshape(2, 128)),
            "xdd": _C(xdg[0:64]), "xdb": _C(xdg[64:80]), "xdc": _C(xdg[80:96]),
        })
    r = pM.run(ins)
    y_all = np.concatenate([_T(r[c]["y"]) for c in range(8)], axis=1)

    pR = _bld("resid", build_resid_ln)
    r = pR.run([{"x": xsh[c], "yt": _C(y_all[c * TS:(c + 1) * TS]),
                 "o2t": o2t[c]} for c in range(8)])
    x1 = [r[c]["x1"] for c in range(8)]
    xn1 = [r[c]["xn1"] for c in range(8)]
    qxn1 = [r[c]["qxn1"] for c in range(8)]
    fcl = [r[c]["fcl"] for c in range(8)]

    inv = 1.0 / (10000.0 ** (np.arange(0, DH_ := 64, 2, dtype=np.float64) / 64.0))
    bqrow = _C(np.asarray(L1_attn_b_qkv, np.float32).reshape(1, 3 * D))
    bgrow = _C(np.asarray(L1_attn_b_gate, np.float32).reshape(1, 32))
    wqkv = _C(np.asarray(L1_attn_w_qkv, np.float32))
    wgate = _C(np.asarray(L1_attn_w_gate, np.float32))
    pQK = _bld("qkv", build_qkv)
    ins = []
    for c in range(8):
        hf = c % 2
        pos = hf * TS + np.arange(TS, dtype=np.float64)
        ang = pos[:, None] * inv[None, :]
        cosv = np.cos(ang).astype(np.float32)
        sinv = np.sin(ang).astype(np.float32)
        ctab = _C(cosv.reshape(8, 128, 32).transpose(1, 0, 2).reshape(128, 256))
        stab = _C(sinv.reshape(8, 128, 32).transpose(1, 0, 2).reshape(128, 256))
        sc8 = np.float32(0.125)
        ins.append({"xn1T": _T(xn1[c]), "wqkv": wqkv, "bqkvrow": bqrow,
                    "wgate": wgate, "bgrow": bgrow,
                    "ctabq": _C(ctab * sc8), "stabq": _C(stab * sc8),
                    "ctabk": ctab, "stabk": stab})
    r = pQK.run(ins)
    q_fm = [_T(r[c]["q_tm"]) for c in range(8)]
    k_fm = [_T(r[c]["k_tm"]) for c in range(8)]
    v_tm = [r[c]["v_tm"] for c in range(8)]
    g1m = [r[c]["g1m"] for c in range(8)]
    gm = [r[c]["gm"] for c in range(8)]

    maskl = np.where(np.tril(np.ones((64, 64), bool)), 0.0, NEG).astype(np.float32)
    pqT = _T(np.asarray(L1_attn_pool_q, np.float32))
    idb = np.eye(128, dtype=BF)
    pLP = _bld("locpool", build_locpool)
    r = pLP.run([{"q_fm": q_fm[c], "k_fm": k_fm[c], "v_tm": v_tm[c], "pqT": pqT,
                  "g1l": _C(g1m[c].reshape(16, 64, 32).transpose(1, 0, 2)
                            .reshape(64, 512)),
                  "maskl": maskl, "identb": idb} for c in range(8)])
    olocal = [r[c]["olocal"] for c in range(8)]
    lmf = [r[c]["lmf"] for c in range(8)]

    wlmqkv = _C(np.asarray(L1_attn_w_lm_qkv, np.float32))
    blmqkvfm = _T(np.asarray(L1_attn_b_lm_qkv, np.float32).reshape(48, 128))
    blvrow = _C(np.asarray(L1_attn_b_lm_qkv, np.float32)[4096:].reshape(1, D))
    wlmout = _C(np.asarray(L1_attn_w_lm_out, np.float32))
    blmoutfm = _T(np.asarray(L1_attn_b_lm_out, np.float32).reshape(16, 128))
    wl2tq = _C(np.asarray(L1_attn_w_l2t_q, np.float32))
    bl2tqfm = _T(np.asarray(L1_attn_b_l2t_q, np.float32).reshape(16, 128))
    wl2tout = _C(np.asarray(L1_attn_w_l2t_out, np.float32))
    bl2toutfm = _T(np.asarray(L1_attn_b_l2t_out, np.float32).reshape(16, 128))
    pLT = _bld("latent", build_latent)
    ins = []
    for c in range(8):
        hf = c % 2
        mate = c ^ 1
        lmfg = _C(np.concatenate([lmf[c], lmf[mate]], axis=1))
        bq_ = hf * 16 + np.arange(64) // 4
        bk_ = np.concatenate([hf * 16 + np.arange(64) // 4,
                              (1 - hf) * 16 + np.arange(64) // 4])
        mlm = np.where(bq_[:, None] >= bk_[None, :], 0.0, NEG).astype(np.float32)
        ins.append({"lmfg": lmfg, "wlmqkv": wlmqkv, "blmqkvfm": blmqkvfm,
                    "blvrow": blvrow, "wlmout": wlmout, "blmoutfm": blmoutfm,
                    "wl2tq": wl2tq, "bl2tqfm": bl2tqfm, "wl2tout": wl2tout,
                    "bl2toutfm": bl2toutfm, "k_fm": k_fm[c], "v_tm": v_tm[c],
                    "masklm": mlm, "identb": idb})
    r = pLT.run(ins)
    lmf3 = [r[c]["lmf3"] for c in range(8)]

    wlmkv = _C(np.asarray(L1_attn_w_lm_kv, np.float32))
    bgkfm = _T(np.asarray(L1_attn_b_lm_kv, np.float32)[:2048].reshape(16, 128))
    bgvrow = _C(np.asarray(L1_attn_b_lm_kv, np.float32)[2048:].reshape(1, D))
    pGL = _bld("global", build_global)
    ins = []
    for c in range(8):
        hf = c % 2
        base = (c // 2) * 2
        lmf3g = _C(np.concatenate([lmf3[base], lmf3[base + 1]], axis=1))
        p_ = np.arange(128)
        maskg = np.zeros((128, 8 * 128), np.float32)
        zr = np.zeros((128, 8), np.float32)
        for tc in range(8):
            tokblk = (hf * TS + tc * 128 + p_) // 64
            latblk = np.arange(128) // 4
            valid = tokblk[:, None] > latblk[None, :]
            maskg[:, tc * 128:(tc + 1) * 128] = np.where(valid, 0.0, NEG)
            zr[:, tc] = (tokblk > 0).astype(np.float32)
        gfm = _C(gm[c].reshape(8, 128, 32).transpose(1, 0, 2).reshape(128, 256))
        ins.append({"lmf3g": lmf3g, "wlmkv": wlmkv, "bgkfm": bgkfm,
                    "bgvrow": bgvrow, "q_fm": q_fm[c], "olocal": olocal[c],
                    "gfm": gfm, "maskg": _C(maskg), "zrec": _C(zr),
                    "identb": idb})
    r = pGL.run(ins)
    oat = [r[c]["oat"] for c in range(8)]

    wout = _C(np.asarray(L1_attn_w_out, np.float32))
    boutfm = _T(np.asarray(L1_attn_b_out, np.float32).reshape(16, 128))
    pWO = _bld("wout", build_wout)
    r = pWO.run([{"oat": oat[c], "wout": wout, "boutfm": boutfm}
                 for c in range(8)])
    attnp = [r[c]["attnp"] for c in range(8)]

    gsub = _C(np.asarray(L1_ffn_subln_weight, np.float32).reshape(1, IF))
    pGLU = _bld("glu", build_glu)
    r = pGLU.run([{"qxn1T": _T(qxn1[c]), "qwgT": qwgT, "qwuT": qwuT,
                   "gsubrow": gsub, "fcl8": _T(fcl[c].reshape(8, 128)),
                   "s3x8": s3x8} for c in range(8)])
    qacts = [r[c]["qacts"] for c in range(8)]
    pDF = _bld("downf", build_down_final)
    r = pDF.run([{"qactsT": _T(qacts[c]), "qwdT": qwdT, "x1T": _T(x1[c]),
                  "attnp": attnp[c]} for c in range(8)])
    out = np.zeros((4, 2048, D), np.float32)
    for c in range(8):
        out[c // 2, (c % 2) * TS:(c % 2 + 1) * TS, :] = _T(r[c]["x2"])
    return out
